# version 13
# baseline (speedup 1.0000x reference)
import contextlib
import numpy as np
import concourse.bass as bass
import concourse.mybir as mybir
from concourse.bass_utils import run_bass_kernel_spmd

F32 = mybir.dt.float32
BF16 = mybir.dt.bfloat16
AX = mybir.AxisListType.X
AF = mybir.ActivationFunctionType
AL = mybir.AluOpType

B, S, E, H, D, INTER = 2, 4096, 2048, 16, 128, 8192
NT = 1024
NCH = E // 128
SCALE = D ** -0.5
SHIFT = -20.0

STREAMS = ["te", "dve", "act", "gps", "syncd", "actd", "gpsd"]
ENG_OF = {"te": "tensor", "dve": "vector", "act": "scalar", "gps": "gpsimd",
          "syncd": "sync", "actd": "scalar", "gpsd": "gpsimd"}


def slot_order(g):
    return [31 - g, 24 + g, 23 - g, 16 + g, 15 - g, 8 + g, 7 - g, g]


def kv_loc(t):
    m, r8 = t // 8, t % 8
    if r8 < 4:
        return r8, (7 - 2 * m) * 128
    return 7 - r8, (6 - 2 * m) * 128


class Res:
    __slots__ = ("w", "rs")

    def __init__(self):
        self.w = None
        self.rs = []


class Plan:
    def __init__(self):
        self.engines = sorted(set(ENG_OF.values()))
        self.steps = {e: [] for e in self.engines}
        self.count = {}
        self.semnames = []

    def step(self, stream, fn, reads=(), writes=(), extra=(), inc=None,
             sem=None, waw=True):
        if sem is None:
            sem = stream
        if sem not in self.count:
            self.count[sem] = 0
            self.semnames.append(sem)
        if inc is None:
            inc = 16 if stream.endswith("d") else 1
        waits = list(extra)
        for r in reads:
            if r.w is not None:
                waits.append(r.w)
        for r in writes:
            if waw and r.w is not None:
                waits.append(r.w)
            waits.extend(r.rs)
        w = {}
        for (s2, v2) in waits:
            if s2 == "te" and stream == "te":
                continue
            w[s2] = max(w.get(s2, 0), v2)
        self.steps[ENG_OF[stream]].append((sorted(w.items()), fn, sem, inc))
        self.count[sem] += inc
        tok = (sem, self.count[sem])
        for r in reads:
            r.rs.append(tok)
        for r in writes:
            r.w, r.rs = tok, []
        return tok

    def build_body(self, sems, engine):
        hiwater = {}

        def body(eng):
            for waits, fn, sem, inc in self.steps[engine]:
                for s2, v2 in waits:
                    if hiwater.get(s2, -1) >= v2:
                        continue
                    eng.wait_ge(sems[s2], v2)
                    hiwater[s2] = v2
                ins = fn(eng)
                ins.then_inc(sems[sem], inc)
        return body


def build_nc():
    nc = bass.Bass("TRN2", target_bir_lowering=False, debug=False)

    def dram(n, s, kind=None, dt=F32):
        t = nc.dram_tensor(n, s, dt, kind=kind) if kind else nc.dram_tensor(n, s, dt)
        return t.ap()
    x_own = dram("x_own", [NT, E], "ExternalInput")
    wqkvT = dram("wqkvT", [E, 3 * E], "ExternalInput")
    woutT = dram("woutT", [E, E], "ExternalInput")
    wguT = dram("wguT", [E, 2 * INTER], "ExternalInput")
    wdT = dram("wdT", [INTER, E], "ExternalInput")
    wmod_sl = dram("wmod_sl", [256, 6 * E], "ExternalInput")
    ct_sl = dram("ct_sl", [256, 2], "ExternalInput")
    cos_h = dram("cos_h", [128, NT], "ExternalInput")
    sin_h = dram("sin_h", [128, NT], "ExternalInput")
    ident_h = dram("ident_h", [128, 128], "ExternalInput")
    ones1_h = dram("ones1_h", [1, 128], "ExternalInput")
    onesc_h = dram("onesc_h", [128, 1], "ExternalInput")
    onesb_h = dram("onesb_h", [128, 1], "ExternalInput", BF16)
    rotT_h = dram("rotT_h", [128, 128], "ExternalInput")
    masks_h = dram("masks_h", [128, 8 * 4 * 128], "ExternalInput", BF16)
    negb_h = dram("negb_h", [128, 1], "ExternalInput")
    sel_h = dram("sel_h", [128, 2], "ExternalInput")
    xfm_scr = dram("xfm_scr", [E, NT])
    x2_scr = dram("x2_scr", [E, NT])
    kag_in = dram("kag_in", [2 * E, NT], dt=BF16)
    kag_out = dram("kag_out", [8 * E, NT], dt=BF16)
    vag_in = dram("vag_in", [2 * H * NT, 128], dt=BF16)
    vag_out = dram("vag_out", [8 * H * NT, 128], dt=BF16)
    mod_in = dram("mod_in", [128, 192])
    mod_out = dram("mod_out", [128, 192])
    h3_scr = dram("h3_scr", [2 * INTER, NT], dt=BF16)
    out_fm = dram("out_fm", [E, NT], "ExternalOutput")

    plan = Plan()
    P = plan.step
    groups_kv = [[0, 1, 2, 3], [4, 5, 6, 7]]
    group_all = [[0, 1, 2, 3, 4, 5, 6, 7]]

    stack = contextlib.ExitStack()

    def SB(n, sh, dt=F32):
        return stack.enter_context(nc.sbuf_tensor("s_" + n, sh, dt))

    PS = lambda n: stack.enter_context(nc.psum_tensor("p_" + n, [128, 1024], F32))

    ident = SB("ident", [128, 128]); ones1 = SB("ones1", [1, 128])
    onesc = SB("onesc", [128, 1]); rotT = SB("rotT", [128, 128])
    onesb = SB("onesb", [128, 1], BF16)
    cosb = SB("cos", [128, NT]); sinb = SB("sin", [128, NT])
    masks = SB("masks", [128, 4096], BF16)
    negb = SB("negb", [128, 1]); selb = SB("sel", [128, 2])
    ctsl = SB("ctsl", [128, 4]); silu_sl = SB("silu", [128, 4])
    modfm = SB("modfm", [128, 192]); ownmod = SB("ownmod", [128, 96])
    modd = SB("modd", [128, 64]); gwork = SB("gwork", [128, 32])
    hbuf = SB("hbuf", [128, 2 * NCH * NT], BF16)
    qbuf = SB("qbuf", [128, 2 * H * NT], BF16)
    wsl = SB("wsl", [128, 2048])
    wsb = SB("wsb", [128, 4096], BF16)
    kvb = SB("kvb", [128, 2816], BF16)
    esl = SB("esl", [128, 2048])
    eslb = SB("eslb", [128, 3072], BF16)
    h3sb = eslb
    ktb = eslb[:, 0:2048]
    vtb = eslb[:, 2048:3072]
    tmpa = SB("tmpa", [128, 2048])
    scrb = SB("scrb", [128, 2048])
    rows = SB("rows", [1, 2048])
    vt4 = scrb[:, 0:512]
    invb = tmpa[:, 0:1024]
    scol = SB("scol", [128, 8]); sqcol = SB("sqcol", [128, 8])
    modp = scrb[:, 1024:1216]

    psA, psB, psC, psZ = PS("A"), PS("B"), PS("C"), PS("Z")
    PSMAP = [psA, psA, psB, psB, psC, psC, psZ, psZ]

    rbank = [Res() for _ in range(8)]
    r_h = [Res() for _ in range(NCH)]
    r_q = [Res() for _ in range(H)]
    r_wsl = [Res() for _ in range(4)]
    r_wsb = [Res() for _ in range(4)]
    r_k = [Res() for _ in range(4)]; r_v = [Res() for _ in range(7)]
    r_esl = [Res(), Res()]
    r_eb = [Res(), Res()]
    r_tmpa = [Res(), Res()]
    r_scr = Res()
    r_rows = Res(); r_scol = Res(); r_sqcol = Res()
    r_consts = Res(); r_mod = Res(); r_own = Res(); r_modd = Res(); r_gw = Res()
    r_silu = Res()
    d_xfm = [Res(), Res()]; d_x2 = Res()
    d_kin = Res(); d_kout = Res(); d_vin = [Res(), Res()]; d_vout = Res()
    d_modin = Res(); d_modout = Res(); d_h3 = Res(); d_out = Res()

    def bank_ap(k, p0=0, p1=128, c0=0, c1=512):
        return PSMAP[k][p0:p1, (k % 2) * 512 + c0:(k % 2) * 512 + c1]

    def wsb_hi(sl, c0, c1):
        return wsb[:, sl * 1024 + c0:sl * 1024 + c1]

    def wsb_lo(sl, c0, c1):
        return wsb[:, sl * 1024 + 512 + c0:sl * 1024 + 512 + c1]

    def split_w(sl, width):
        P("dve", lambda e, sl=sl, width=width: e.tensor_copy(
            wsb_hi(sl, 0, width), wsl[:, sl * 512:sl * 512 + width]),
          reads=[r_wsl[sl]], writes=[r_wsb[sl]])
        P("dve", lambda e, sl=sl, width=width: e.tensor_sub(
            wsb_lo(sl, 0, width), wsl[:, sl * 512:sl * 512 + width], wsb_hi(sl, 0, width)),
          reads=[r_wsl[sl]], writes=[r_wsb[sl]])

    for dst, src in [(ident[:], ident_h[:]), (ones1[:], ones1_h[:]),
                     (onesc[:], onesc_h[:]), (onesb[:], onesb_h[:]),
                     (rotT[:], rotT_h[:]),
                     (cosb[:], cos_h[:]), (sinb[:], sin_h[:]),
                     (masks[:], masks_h[:]), (negb[:], negb_h[:]),
                     (selb[:], sel_h[:])]:
        P("syncd", lambda e, a=dst, b=src: e.dma_start(out=a, in_=b),
          writes=[r_consts], sem="dconst")
    for kc in range(2):
        P("gpsd", lambda e, kc=kc: e.dma_start(out=ctsl[:, kc * 2:kc * 2 + 2],
                                               in_=ct_sl[kc * 128:(kc + 1) * 128, :]),
          writes=[r_consts], sem="dct")

    P("act", lambda e: e.activation(silu_sl[:], ctsl[:], AF.Silu),
      reads=[r_consts], writes=[r_silu])
    for j in range(96):
        sl = j % 4
        P("syncd", lambda e, j=j, sl=sl: e.dma_start(
            out=wsl[:, sl * 512:sl * 512 + 256],
            in_=wmod_sl[:, j * 128:(j + 1) * 128].rearrange("(kc p) j -> p kc j", p=128)),
          writes=[r_wsl[sl]], sem="dw%d" % sl)

        def modmm(e, j=j, sl=sl):
            pt = psA if j < 48 else psB
            col = (j % 48) * 2
            for kc in range(2):
                ins = e.matmul(pt[:, col:col + 2],
                               wsl[:, sl * 512 + kc * 128:sl * 512 + (kc + 1) * 128],
                               silu_sl[:, kc * 2:kc * 2 + 2],
                               start=(kc == 0), stop=(kc == 1))
            return ins
        P("te", modmm, reads=[r_wsl[sl], r_silu], writes=[rbank[0 if j < 48 else 2]])
    P("dve", lambda e: e.tensor_copy(modp[:, 0:96], psA[:, 0:96]),
      reads=[rbank[0]], writes=[r_scr])
    P("dve", lambda e: e.tensor_copy(modp[:, 96:192], psB[:, 0:96]),
      reads=[rbank[2]], writes=[r_scr])
    P("gpsd", lambda e: e.dma_start(out=mod_in[:], in_=modp[:]),
      reads=[r_scr], writes=[d_modin], sem="dmodin")
    P("gps", lambda e: e.collective_compute(
        "AllReduce", AL.add, replica_groups=group_all,
        ins=[mod_in[:]], outs=[mod_out[:]]),
      reads=[d_modin], writes=[d_modout])
    P("gpsd", lambda e: e.dma_start(out=modfm[:], in_=mod_out[:]),
      reads=[d_modout], writes=[r_mod], sem="dmodld")
    P("dve", lambda e: e.tensor_scalar(scrb[:, 0:96], modfm[:, 0:192:2],
                                       selb[:, 0:1], None, op0=AL.mult),
      reads=[r_mod, r_consts], writes=[r_scr])
    P("dve", lambda e: e.scalar_tensor_tensor(ownmod[:], modfm[:, 1:192:2],
                                              selb[:, 1:2], scrb[:, 0:96],
                                              op0=AL.mult, op1=AL.add),
      reads=[r_mod, r_consts, r_scr], writes=[r_own])
    P("dve", lambda e: e.tensor_scalar_add(modd[:, 0:16], ownmod[:, 16:32], 1.0),
      reads=[r_own], writes=[r_modd])
    P("dve", lambda e: e.tensor_scalar_add(modd[:, 16:32], ownmod[:, 64:80], 1.0),
      reads=[r_own], writes=[r_modd])
    for qq, base in ((2, 32), (5, 48)):
        P("dve", lambda e, qq=qq: e.tensor_single_scalar(gwork[:, 0:16], ownmod[:, qq * 16:(qq + 1) * 16], 0.0, op=AL.is_ge),
          reads=[r_own], writes=[r_gw])
        P("dve", lambda e: e.tensor_scalar(gwork[:, 0:16], gwork[:, 0:16], 2.0, -1.0, op0=AL.mult, op1=AL.add),
          reads=[r_gw], writes=[r_gw])
        P("act", lambda e, qq=qq: e.activation(gwork[:, 16:32], ownmod[:, qq * 16:(qq + 1) * 16], AF.Abs),
          reads=[r_own], writes=[r_gw])
        P("dve", lambda e: e.tensor_scalar_max(gwork[:, 16:32], gwork[:, 16:32], 0.25),
          reads=[r_gw], writes=[r_gw])
        P("dve", lambda e, base=base: e.tensor_mul(modd[:, base:base + 16], gwork[:, 0:16], gwork[:, 16:32]),
          reads=[r_gw], writes=[r_modd])

    sa_ap = lambda ch: ownmod[:, ch:ch + 1]
    sb_ap = lambda ch: ownmod[:, 48 + ch:48 + ch + 1]
    ca1_ap = lambda ch: modd[:, ch:ch + 1]
    cb1_ap = lambda ch: modd[:, 16 + ch:16 + ch + 1]
    ga_ap = lambda ch: modd[:, 32 + ch:32 + ch + 1]
    gb_ap = lambda ch: modd[:, 48 + ch:48 + ch + 1]

    for til in range(8):
        xs = esl[:, 0:2048]
        P("actd", lambda e, til=til, xs=xs: e.dma_start(out=xs, in_=x_own[til * 128:(til + 1) * 128, :]),
          writes=[r_esl[0], r_esl[1]], sem="dx0")
        P("dve", lambda e, til=til, xs=xs: e.reduce_sum(scol[:, til:til + 1], xs, axis=AX),
          reads=[r_esl[0], r_esl[1]], writes=[r_scol])
        P("act", lambda e, til=til, xs=xs: e.activation(scrb[:], xs, AF.Square, accum_out=sqcol[:, til:til + 1]),
          reads=[r_esl[0], r_esl[1]], writes=[r_scr, r_sqcol])
        for bb in range(4):
            def trn(e, til=til, xs=xs, bb=bb):
                for k in range(4):
                    ch = bb * 4 + k
                    ins = e.transpose(bank_ap(bb % 4, c0=k * 128, c1=(k + 1) * 128),
                                      xs[:, ch * 128:(ch + 1) * 128], ident[:])
                return ins
            P("te", trn, reads=[r_esl[0], r_esl[1], r_consts], writes=[rbank[bb % 4]])
            P("dve", lambda e, bb=bb: e.tensor_copy(tmpa[:, (bb % 2) * 1024:(bb % 2) * 1024 + 512],
                                                    bank_ap(bb % 4)),
              reads=[rbank[bb % 4]], writes=[r_tmpa[bb % 2]])
            for k in range(4):
                ch = bb * 4 + k
                P("gpsd", lambda e, til=til, ch=ch, bb=bb, k=k: e.dma_start(
                    out=xfm_scr[ch * 128:(ch + 1) * 128, til * 128:(til + 1) * 128],
                    in_=tmpa[:, (bb % 2) * 1024 + k * 128:(bb % 2) * 1024 + (k + 1) * 128]),
                  reads=[r_tmpa[bb % 2]], writes=[d_xfm[bb % 2]], sem="dxfm%d" % (bb % 2), waw=False)
    for til in range(8):
        P("te", lambda e, til=til: e.transpose(bank_ap(4 + til // 4, p1=1, c0=(til % 4) * 128, c1=(til % 4 + 1) * 128),
                                               scol[:, til:til + 1], ident[:]),
          reads=[r_scol, r_consts], writes=[rbank[4 + til // 4]])
        P("te", lambda e, til=til: e.transpose(bank_ap(6 + til // 4, p1=1, c0=(til % 4) * 128, c1=(til % 4 + 1) * 128),
                                               sqcol[:, til:til + 1], ident[:]),
          reads=[r_sqcol, r_consts], writes=[rbank[6 + til // 4]])

    def stats_rows(sum_banks, sq_banks, sum_src, sq_src):
        P("act", lambda e: e.mul(rows[:, 0:1024], sum_src, 1.0 / 2048.0),
          reads=sum_banks, writes=[r_rows])
        P("act", lambda e: e.mul(rows[:, 1024:2048], sq_src, 1.0 / 2048.0),
          reads=sq_banks, writes=[r_rows])
        P("act", lambda e: e.activation(scrb[0:1, 0:1024], rows[:, 0:1024], AF.Square),
          reads=[r_rows], writes=[r_scr])
        P("dve", lambda e: e.tensor_sub(rows[:, 1024:2048], rows[:, 1024:2048], scrb[0:1, 0:1024]),
          reads=[r_scr], writes=[r_rows])
        P("dve", lambda e: e.tensor_scalar_add(rows[:, 1024:2048], rows[:, 1024:2048], 1e-5),
          reads=[r_rows], writes=[r_rows])
        P("act", lambda e: e.activation(scrb[0:1, 0:1024], rows[:, 1024:2048], AF.Sqrt),
          reads=[r_rows], writes=[r_scr])
        P("dve", lambda e: e.reciprocal(rows[:, 1024:2048], scrb[0:1, 0:1024]),
          reads=[r_scr], writes=[r_rows])

    stats_rows([rbank[4], rbank[5]], [rbank[6], rbank[7]],
               psC[0:1, 0:1024], psZ[0:1, 0:1024])

    def bcast_rows():
        for half in range(2):
            P("te", lambda e, half=half: e.matmul(
                psA[:, half * 512:(half + 1) * 512], ones1[:],
                rows[0:1, half * 512:(half + 1) * 512], start=True, stop=True),
              reads=[r_rows, r_consts], writes=[rbank[half]])
            P("te", lambda e, half=half: e.matmul(
                psB[:, half * 512:(half + 1) * 512], ones1[:],
                rows[0:1, 1024 + half * 512:1024 + (half + 1) * 512], start=True, stop=True),
              reads=[r_rows, r_consts], writes=[rbank[2 + half]])

    def ln_apply(c1ap, shap, dst_res, dst_hi, dst_lo, src_res, src_ap):
        P("dve", lambda e: e.tensor_sub(scrb[:, 0:1024], src_ap, psA[:, 0:1024]),
          reads=[src_res, rbank[0], rbank[1]], writes=[r_scr])
        P("dve", lambda e: e.tensor_mul(scrb[:, 1024:2048], scrb[:, 0:1024], psB[:, 0:1024]),
          reads=[r_scr, rbank[2], rbank[3]], writes=[r_scr])
        P("dve", lambda e: e.tensor_scalar(scrb[:, 0:1024], scrb[:, 1024:2048], c1ap, shap,
                                           op0=AL.mult, op1=AL.add),
          reads=[r_scr, r_modd, r_own], writes=[r_scr])
        P("dve", lambda e: e.tensor_copy(dst_hi, scrb[:, 0:1024]),
          reads=[r_scr], writes=[dst_res])
        P("dve", lambda e: e.tensor_sub(dst_lo, scrb[:, 0:1024], dst_hi),
          reads=[r_scr], writes=[dst_res])

    bcast_rows()
    for ch in range(NCH):
        sl = ch % 2
        xs = esl[:, sl * 1024:(sl + 1) * 1024]
        P("syncd", lambda e, ch=ch, xs=xs: e.dma_start(out=xs, in_=xfm_scr[ch * 128:(ch + 1) * 128, :]),
          reads=d_xfm, writes=[r_esl[sl]], sem="de%ds" % sl)
        ln_apply(ca1_ap(ch), sa_ap(ch), r_h[ch],
                 hbuf[:, ch * 2048:ch * 2048 + 1024],
                 hbuf[:, ch * 2048 + 1024:ch * 2048 + 2048],
                 r_esl[sl], xs)

    def mm3(e, out_ap, whi, wlo, xhi, xlo, first, last):
        e.matmul(out_ap, whi, xhi, start=first, stop=False)
        e.matmul(out_ap, whi, xlo, start=False, stop=False)
        ins = e.matmul(out_ap, wlo, xhi, start=False, stop=last)
        return ins

    def rope_chain(raw_res, raw_ap):
        for half in range(2):
            P("te", lambda e, half=half, raw_ap=raw_ap: e.matmul(
                bank_ap(6 + half), rotT[:], raw_ap[:, half * 512:(half + 1) * 512],
                start=True, stop=True),
              reads=[raw_res, r_consts], writes=[rbank[6 + half]])
        P("dve", lambda e: e.tensor_mul(scrb[:, 0:1024], psZ[:, 0:1024], sinb[:]),
          reads=[rbank[6], rbank[7], r_consts], writes=[r_scr])
        P("dve", lambda e, raw_ap=raw_ap: e.tensor_mul(scrb[:, 1024:2048], raw_ap, cosb[:]),
          reads=[raw_res, r_consts], writes=[r_scr])
        P("dve", lambda e: e.tensor_add(scrb[:, 0:1024], scrb[:, 0:1024], scrb[:, 1024:2048]),
          reads=[r_scr], writes=[r_scr])

    OG_ORDER = [11, 12, 13, 14, 15, 10, 9, 8, 7, 6, 5, 4, 3, 2, 1, 0]
    for og in OG_ORDER:
        ochs = [og * 3 + i for i in range(3)]
        for kch in range(NCH):
            sl = kch % 4
            P("syncd", lambda e, og=og, kch=kch, sl=sl: e.dma_start(
                out=wsl[:, sl * 512:sl * 512 + 384],
                in_=wqkvT[kch * 128:(kch + 1) * 128, og * 384:(og + 1) * 384]),
              writes=[r_wsl[sl]], sem="dw%d" % sl)
            split_w(sl, 384)

            def qkvmm(e, kch=kch, sl=sl):
                for oi in range(3):
                    for half in range(2):
                        ins = mm3(e,
                                  PSMAP[2 * oi][:, half * 512:(half + 1) * 512],
                                  wsb_hi(sl, oi * 128, (oi + 1) * 128),
                                  wsb_lo(sl, oi * 128, (oi + 1) * 128),
                                  hbuf[:, kch * 2048 + half * 512:kch * 2048 + (half + 1) * 512],
                                  hbuf[:, kch * 2048 + 1024 + half * 512:kch * 2048 + 1024 + (half + 1) * 512],
                                  first=(kch == 0), last=(kch == NCH - 1))
                return ins
            P("te", qkvmm, reads=[r_wsb[sl], r_h[kch]],
              writes=[rbank[0], rbank[1], rbank[2], rbank[3], rbank[4], rbank[5]])
        for oi, och in enumerate(ochs):
            asl = oi % 2
            raw = tmpa[:, asl * 1024:(asl + 1) * 1024]
            pX = PSMAP[2 * oi]
            P("dve", lambda e, pX=pX, raw=raw: e.tensor_copy(raw, pX[:, 0:1024]),
              reads=[rbank[2 * oi], rbank[2 * oi + 1]], writes=[r_tmpa[asl]])
            if och < 16:
                hh = och
                rope_chain(r_tmpa[asl], raw)
                P("dve", lambda e, hh=hh: e.tensor_copy(qbuf[:, hh * 2048:hh * 2048 + 1024], scrb[:, 0:1024]),
                  reads=[r_scr], writes=[r_q[hh]])
                P("dve", lambda e, hh=hh: e.tensor_sub(qbuf[:, hh * 2048 + 1024:hh * 2048 + 2048],
                                                       scrb[:, 0:1024], qbuf[:, hh * 2048:hh * 2048 + 1024]),
                  reads=[r_scr], writes=[r_q[hh]])
            elif och < 32:
                hh = och - 16
                rope_chain(r_tmpa[asl], raw)
                P("dve", lambda e: e.tensor_copy(ktb[:, 0:1024], scrb[:, 0:1024]),
                  reads=[r_scr], writes=[r_eb[0]])
                P("dve", lambda e: e.tensor_sub(ktb[:, 1024:2048], scrb[:, 0:1024], ktb[:, 0:1024]),
                  reads=[r_scr], writes=[r_eb[0]])
                P("gpsd", lambda e, hh=hh: e.dma_start(
                    out=kag_in[hh * 128:(hh + 1) * 128, :], in_=ktb[:, 0:1024]),
                  reads=[r_eb[0]], writes=[d_kin], sem="dkin", waw=False)
                P("gpsd", lambda e, hh=hh: e.dma_start(
                    out=kag_in[E + hh * 128:E + (hh + 1) * 128, :], in_=ktb[:, 1024:2048]),
                  reads=[r_eb[0]], writes=[d_kin], sem="dkin", waw=False)
            else:
                hh = och - 32
                for grp in range(2):
                    def vtr(e, raw=raw, grp=grp):
                        for k in range(4):
                            ins = e.transpose(bank_ap(6 + grp, c0=k * 128, c1=(k + 1) * 128),
                                              raw[:, (grp * 4 + k) * 128:(grp * 4 + k + 1) * 128],
                                              ident[:])
                        return ins
                    P("te", vtr, reads=[r_tmpa[asl], r_consts], writes=[rbank[6 + grp]])
                    P("dve", lambda e, grp=grp: e.tensor_copy(vt4[:], bank_ap(6 + grp)),
                      reads=[rbank[6 + grp]], writes=[r_scr])
                    P("dve", lambda e: e.tensor_copy(vtb[:, 0:512], vt4[:]),
                      reads=[r_scr], writes=[r_eb[1]])
                    P("dve", lambda e: e.tensor_sub(vtb[:, 512:1024], vt4[:], vtb[:, 0:512]),
                      reads=[r_scr], writes=[r_eb[1]])
                    for k in range(4):
                        sub = grp * 4 + k
                        P("gpsd", lambda e, hh=hh, sub=sub, k=k: e.dma_start(
                            out=vag_in[hh * NT + sub * 128:hh * NT + (sub + 1) * 128, :],
                            in_=vtb[:, k * 128:(k + 1) * 128]),
                          reads=[r_eb[1]], writes=[d_vin[grp]], sem="dvin%d" % grp, waw=False)
                        P("gpsd", lambda e, hh=hh, sub=sub, k=k: e.dma_start(
                            out=vag_in[H * NT + hh * NT + sub * 128:H * NT + hh * NT + (sub + 1) * 128, :],
                            in_=vtb[:, 512 + k * 128:512 + (k + 1) * 128]),
                          reads=[r_eb[1]], writes=[d_vin[grp]], sem="dvin%d" % grp, waw=False)
        if og == 10:
            P("gps", lambda e: e.collective_compute(
                "AllGather", AL.bypass, replica_groups=groups_kv,
                ins=[vag_in[:]], outs=[vag_out[:]]),
              reads=d_vin, writes=[d_vout])
        if og == 5:
            P("gps", lambda e: e.collective_compute(
                "AllGather", AL.bypass, replica_groups=groups_kv,
                ins=[kag_in[:]], outs=[kag_out[:]]),
              reads=[d_kin], writes=[d_kout])

    KBLK, VBLK = 2 * E, 2 * H * NT

    def eb_hi(sl3, c0, c1):
        return hbuf[:, sl3 * 2048 + c0:sl3 * 2048 + c1]

    def eb_lo(sl3, c0, c1):
        return hbuf[:, sl3 * 2048 + 1024 + c0:sl3 * 2048 + 1024 + c1]

    for hh in range(H):
        def front(kt, hh=hh):
            w = 128 * (8 - kt // 4)
            par = kt % 2
            sl3 = kt % 5
            ksl, vsl = kt % 4, kt % 7
            rank, col0 = kv_loc(kt)
            P("syncd", lambda e, hh=hh, rank=rank, col0=col0, ksl=ksl: e.dma_start(
                out=kvb[:, ksl * 256:ksl * 256 + 128],
                in_=kag_out[rank * KBLK + hh * 128:rank * KBLK + (hh + 1) * 128, col0:col0 + 128]),
              reads=[d_kout], writes=[r_k[ksl]], sem="dk%d" % ksl)
            P("syncd", lambda e, hh=hh, rank=rank, col0=col0, ksl=ksl: e.dma_start(
                out=kvb[:, ksl * 256 + 128:ksl * 256 + 256],
                in_=kag_out[rank * KBLK + E + hh * 128:rank * KBLK + E + (hh + 1) * 128, col0:col0 + 128]),
              reads=[d_kout], writes=[r_k[ksl]], sem="dk%d" % ksl)
            P("gpsd", lambda e, hh=hh, rank=rank, col0=col0, vsl=vsl: e.dma_start(
                out=kvb[:, 1024 + vsl * 256:1024 + vsl * 256 + 128],
                in_=vag_out[rank * VBLK + hh * NT + col0:rank * VBLK + hh * NT + col0 + 128, :]),
              reads=[d_vout], writes=[r_v[vsl]], sem="dv%d" % vsl)
            P("gpsd", lambda e, hh=hh, rank=rank, col0=col0, vsl=vsl: e.dma_start(
                out=kvb[:, 1024 + vsl * 256 + 128:1024 + vsl * 256 + 256],
                in_=vag_out[rank * VBLK + H * NT + hh * NT + col0:rank * VBLK + H * NT + hh * NT + col0 + 128, :]),
              reads=[d_vout], writes=[r_v[vsl]], sem="dv%d" % vsl)
            scp = psA if par == 0 else psB
            scb = [rbank[0], rbank[1]] if par == 0 else [rbank[2], rbank[3]]
            nh = 2 if w > 512 else 1

            def score(e, hh=hh, kt=kt, w=w, scp=scp, ksl=ksl):
                for half in range(2 if w > 512 else 1):
                    ww = min(512, w - half * 512)
                    ins = mm3(e, scp[:, half * 512:half * 512 + ww],
                              kvb[:, ksl * 256:ksl * 256 + 128],
                              kvb[:, ksl * 256 + 128:ksl * 256 + 256],
                              qbuf[:, hh * 2048 + half * 512:hh * 2048 + half * 512 + ww],
                              qbuf[:, hh * 2048 + 1024 + half * 512:hh * 2048 + 1024 + half * 512 + ww],
                              first=True, last=True)
                return ins
            P("te", score, reads=[r_k[ksl], r_q[hh]], writes=scb[:nh])
            P("act", lambda e, par=par, w=w, scp=scp: e.activation(
                esl[:, par * 1024:par * 1024 + w], scp[:, 0:w], AF.Exp,
                bias=negb[:, 0:1], scale=SCALE),
              reads=scb[:nh] + [r_consts], writes=[r_esl[par]])
            s_m, j_m = 7 - kt // 4, kt % 4
            P("dve", lambda e, par=par, s_m=s_m, j_m=j_m: e.tensor_mul(
                esl[:, par * 1024 + s_m * 128:par * 1024 + (s_m + 1) * 128],
                esl[:, par * 1024 + s_m * 128:par * 1024 + (s_m + 1) * 128],
                masks[:, (s_m * 4 + j_m) * 128:(s_m * 4 + j_m + 1) * 128]),
              reads=[r_consts], writes=[r_esl[par]])
            P("dve", lambda e, par=par, w=w, sl3=sl3: e.tensor_copy(
                eb_hi(sl3, 0, w), esl[:, par * 1024:par * 1024 + w]),
              reads=[r_esl[par]], writes=[r_h[sl3]])
            P("dve", lambda e, par=par, w=w, sl3=sl3: e.tensor_sub(
                eb_lo(sl3, 0, w), esl[:, par * 1024:par * 1024 + w], eb_hi(sl3, 0, w)),
              reads=[r_esl[par]], writes=[r_h[sl3]])

        def back(kt, hh=hh):
            w = 128 * (8 - kt // 4)
            sl3 = kt % 5
            vsl = kt % 7

            def colsum(e, kt=kt, w=w, sl3=sl3):
                for half in range(2 if w > 512 else 1):
                    ww = min(512, w - half * 512)
                    last = (kt == 31 if half == 0 else kt == 15)
                    e.matmul(psZ[0:1, half * 512:half * 512 + ww], onesb[:],
                             eb_hi(sl3, half * 512, half * 512 + ww),
                             start=(kt == 0), stop=False)
                    ins = e.matmul(psZ[0:1, half * 512:half * 512 + ww], onesb[:],
                                   eb_lo(sl3, half * 512, half * 512 + ww),
                                   start=False, stop=last)
                return ins
            P("te", colsum, reads=[r_h[sl3], r_consts], writes=[rbank[6]] + ([rbank[7]] if w > 512 else []))

            def avmm(e, kt=kt, w=w, sl3=sl3, vsl=vsl):
                for half in range(2 if w > 512 else 1):
                    ww = min(512, w - half * 512)
                    last = (kt == 31 if half == 0 else kt == 15)
                    ins = mm3(e, psC[:, half * 512:half * 512 + ww],
                              kvb[:, 1024 + vsl * 256:1024 + vsl * 256 + 128],
                              kvb[:, 1024 + vsl * 256 + 128:1024 + vsl * 256 + 256],
                              eb_hi(sl3, half * 512, half * 512 + ww),
                              eb_lo(sl3, half * 512, half * 512 + ww),
                              first=(kt == 0), last=last)
                return ins
            P("te", avmm, reads=[r_h[sl3], r_v[vsl]], writes=[rbank[4]] + ([rbank[5]] if w > 512 else []))

        for kt in range(36):
            if kt < 32:
                front(kt)
            if kt >= 4:
                back(kt - 4)
        P("dve", lambda e: e.reciprocal(scrb[0:1, 0:1024], psZ[0:1, 0:1024]),
          reads=[rbank[6], rbank[7]], writes=[r_scr])
        for half in range(2):
            P("te", lambda e, half=half: e.matmul(
                psA[:, half * 512:(half + 1) * 512], ones1[:],
                scrb[0:1, half * 512:(half + 1) * 512], start=True, stop=True),
              reads=[r_scr, r_consts], writes=[rbank[half]])
        P("dve", lambda e: e.tensor_copy(invb[:], psA[:, 0:1024]),
          reads=[rbank[0], rbank[1]], writes=[r_tmpa[0]])
        P("dve", lambda e: e.tensor_mul(tmpa[:, 1024:2048], psC[:, 0:1024], invb[:]),
          reads=[rbank[4], rbank[5], r_tmpa[0]], writes=[r_tmpa[1]])
        P("dve", lambda e, hh=hh: e.tensor_copy(qbuf[:, hh * 2048:hh * 2048 + 1024], tmpa[:, 1024:2048]),
          reads=[r_tmpa[1]], writes=[r_q[hh]])
        P("dve", lambda e, hh=hh: e.tensor_sub(qbuf[:, hh * 2048 + 1024:hh * 2048 + 2048],
                                               tmpa[:, 1024:2048], qbuf[:, hh * 2048:hh * 2048 + 1024]),
          reads=[r_tmpa[1]], writes=[r_q[hh]])

    for og in range(8):
        pbase = 0 if og % 2 == 0 else 4
        for kch in range(NCH):
            sl = kch % 4
            P("syncd", lambda e, og=og, kch=kch, sl=sl: e.dma_start(
                out=wsl[:, sl * 512:sl * 512 + 256],
                in_=woutT[kch * 128:(kch + 1) * 128, og * 256:(og + 1) * 256]),
              writes=[r_wsl[sl]], sem="dw%d" % sl)
            split_w(sl, 256)

            def womm(e, kch=kch, sl=sl, pbase=pbase):
                for oi in range(2):
                    for half in range(2):
                        ins = mm3(e,
                                  PSMAP[pbase + 2 * oi][:, half * 512:(half + 1) * 512],
                                  wsb_hi(sl, oi * 128, (oi + 1) * 128),
                                  wsb_lo(sl, oi * 128, (oi + 1) * 128),
                                  qbuf[:, kch * 2048 + half * 512:kch * 2048 + (half + 1) * 512],
                                  qbuf[:, kch * 2048 + 1024 + half * 512:kch * 2048 + 1024 + (half + 1) * 512],
                                  first=(kch == 0), last=(kch == NCH - 1))
                return ins
            P("te", womm, reads=[r_wsb[sl], r_q[kch]],
              writes=[rbank[pbase + i] for i in range(4)])
        for oi in range(2):
            ech = og * 2 + oi
            sl = ech % 2
            xs = esl[:, sl * 1024:(sl + 1) * 1024]
            P("syncd", lambda e, ech=ech, xs=xs: e.dma_start(out=xs, in_=xfm_scr[ech * 128:(ech + 1) * 128, :]),
              reads=d_xfm, writes=[r_esl[sl]], sem="de%ds" % sl)
            P("dve", lambda e, ech=ech, oi=oi, pbase=pbase, xs=xs: e.scalar_tensor_tensor(
                scrb[:, 0:1024],
                PSMAP[pbase + 2 * oi][:, 0:1024], ga_ap(ech), xs,
                op0=AL.mult, op1=AL.add),
              reads=[rbank[pbase + 2 * oi], rbank[pbase + 2 * oi + 1], r_esl[sl], r_modd],
              writes=[r_scr])
            P("gpsd", lambda e, ech=ech: e.dma_start(
                out=x2_scr[ech * 128:(ech + 1) * 128, :], in_=scrb[:, 0:1024]),
              reads=[r_scr], writes=[d_x2], sem="dx2", waw=False)

    for ch in range(NCH):
        sl = ch % 2
        xs = esl[:, sl * 1024:(sl + 1) * 1024]
        P("syncd", lambda e, ch=ch, xs=xs: e.dma_start(out=xs, in_=x2_scr[ch * 128:(ch + 1) * 128, :]),
          reads=[d_x2], writes=[r_esl[sl]], sem="de%ds" % sl)
        P("te", lambda e, ch=ch, xs=xs: e.matmul(psZ[0:1, 0:512], onesc[:], xs[:, 0:512],
                                                 start=(ch == 0), stop=(ch == NCH - 1)),
          reads=[r_esl[sl], r_consts], writes=[rbank[6]])
        P("te", lambda e, ch=ch, xs=xs: e.matmul(psZ[0:1, 512:1024], onesc[:], xs[:, 512:1024],
                                                 start=(ch == 0), stop=(ch == NCH - 1)),
          reads=[r_esl[sl], r_consts], writes=[rbank[7]])
        P("act", lambda e, ch=ch, xs=xs: e.activation(scrb[:, 0:1024], xs, AF.Square),
          reads=[r_esl[sl]], writes=[r_scr])
        P("te", lambda e, ch=ch: e.matmul(psC[0:1, 0:512], onesc[:], scrb[:, 0:512],
                                          start=(ch == 0), stop=(ch == NCH - 1)),
          reads=[r_scr, r_consts], writes=[rbank[4]])
        P("te", lambda e, ch=ch: e.matmul(psC[0:1, 512:1024], onesc[:], scrb[:, 512:1024],
                                          start=(ch == 0), stop=(ch == NCH - 1)),
          reads=[r_scr, r_consts], writes=[rbank[5]])
    stats_rows([rbank[6], rbank[7]], [rbank[4], rbank[5]],
               psZ[0:1, 0:1024], psC[0:1, 0:1024])
    bcast_rows()
    for ch in range(NCH):
        sl = ch % 2
        xs = esl[:, sl * 1024:(sl + 1) * 1024]
        P("syncd", lambda e, ch=ch, xs=xs: e.dma_start(out=xs, in_=x2_scr[ch * 128:(ch + 1) * 128, :]),
          reads=[d_x2], writes=[r_esl[sl]], sem="de%ds" % sl)
        ln_apply(cb1_ap(ch), sb_ap(ch), r_h[ch],
                 hbuf[:, ch * 2048:ch * 2048 + 1024],
                 hbuf[:, ch * 2048 + 1024:ch * 2048 + 2048],
                 r_esl[sl], xs)

    for i in range(64):
        par = i % 2
        gb_, ub_ = (0, 2) if par == 0 else (4, 6)
        for kch in range(NCH):
            sl = (i * NCH + kch) % 4
            P("syncd", lambda e, i=i, kch=kch, sl=sl: e.dma_start(
                out=wsl[:, sl * 512:sl * 512 + 128],
                in_=wguT[kch * 128:(kch + 1) * 128, i * 128:(i + 1) * 128]),
              writes=[r_wsl[sl]], sem="dw%d" % sl)
            P("syncd", lambda e, i=i, kch=kch, sl=sl: e.dma_start(
                out=wsl[:, sl * 512 + 128:sl * 512 + 256],
                in_=wguT[kch * 128:(kch + 1) * 128, INTER + i * 128:INTER + (i + 1) * 128]),
              writes=[r_wsl[sl]], sem="dw%d" % sl)
            split_w(sl, 256)

            def gumm(e, kch=kch, sl=sl, gb_=gb_, ub_=ub_):
                for half in range(2):
                    xhi = hbuf[:, kch * 2048 + half * 512:kch * 2048 + (half + 1) * 512]
                    xlo = hbuf[:, kch * 2048 + 1024 + half * 512:kch * 2048 + 1024 + (half + 1) * 512]
                    mm3(e, PSMAP[gb_][:, half * 512:(half + 1) * 512],
                        wsb_hi(sl, 0, 128), wsb_lo(sl, 0, 128), xhi, xlo,
                        first=(kch == 0), last=(kch == NCH - 1))
                    ins = mm3(e, PSMAP[ub_][:, half * 512:(half + 1) * 512],
                              wsb_hi(sl, 128, 256), wsb_lo(sl, 128, 256), xhi, xlo,
                              first=(kch == 0), last=(kch == NCH - 1))
                return ins
            P("te", gumm, reads=[r_wsb[sl], r_h[kch]],
              writes=[rbank[gb_], rbank[gb_ + 1], rbank[ub_], rbank[ub_ + 1]])
        sil = tmpa[:, par * 1024:(par + 1) * 1024]
        P("act", lambda e, gb_=gb_, sil=sil: e.activation(sil, PSMAP[gb_][:, 0:1024], AF.Silu),
          reads=[rbank[gb_], rbank[gb_ + 1]], writes=[r_tmpa[par]])
        h3ap = scrb[:, par * 1024:(par + 1) * 1024]
        P("dve", lambda e, ub_=ub_, sil=sil, h3ap=h3ap: e.tensor_mul(h3ap, sil, PSMAP[ub_][:, 0:1024]),
          reads=[r_tmpa[par], rbank[ub_], rbank[ub_ + 1]], writes=[r_scr])
        P("dve", lambda e, par=par, h3ap=h3ap: e.tensor_copy(qbuf[:, par * 2048:par * 2048 + 1024], h3ap),
          reads=[r_scr], writes=[r_q[par]])
        P("dve", lambda e, par=par, h3ap=h3ap: e.tensor_sub(
            qbuf[:, par * 2048 + 1024:par * 2048 + 2048], h3ap, qbuf[:, par * 2048:par * 2048 + 1024]),
          reads=[r_scr], writes=[r_q[par]])
        P("gpsd", lambda e, i=i, par=par: e.dma_start(
            out=h3_scr[i * 128:(i + 1) * 128, :], in_=qbuf[:, par * 2048:par * 2048 + 1024]),
          reads=[r_q[par]], writes=[d_h3], sem="dh3", waw=False)
        P("gpsd", lambda e, i=i, par=par: e.dma_start(
            out=h3_scr[INTER + i * 128:INTER + (i + 1) * 128, :],
            in_=qbuf[:, par * 2048 + 1024:par * 2048 + 2048]),
          reads=[r_q[par]], writes=[d_h3], sem="dh3", waw=False)
    for eg in range(8):
        pbase = 0 if eg % 2 == 0 else 4
        for ich in range(64):
            wslot = ich % 4
            P("syncd", lambda e, eg=eg, ich=ich, wslot=wslot: e.dma_start(
                out=wsl[:, wslot * 512:wslot * 512 + 256],
                in_=wdT[ich * 128:(ich + 1) * 128, eg * 256:(eg + 1) * 256]),
              writes=[r_wsl[wslot]], sem="dw%d" % wslot)
            split_w(wslot, 256)
            hsl = ich % 8
            P("actd", lambda e, ich=ich, hsl=hsl: e.dma_start(
                out=hbuf[:, hsl * 2048:hsl * 2048 + 1024],
                in_=h3_scr[ich * 128:(ich + 1) * 128, :]),
              reads=[d_h3], writes=[r_h[hsl]], sem="de%da" % hsl)
            P("actd", lambda e, ich=ich, hsl=hsl: e.dma_start(
                out=hbuf[:, hsl * 2048 + 1024:hsl * 2048 + 2048],
                in_=h3_scr[INTER + ich * 128:INTER + (ich + 1) * 128, :]),
              reads=[d_h3], writes=[r_h[hsl]], sem="de%da" % hsl)

            def dmm(e, ich=ich, wslot=wslot, pbase=pbase, hsl=hsl):
                for oi in range(2):
                    for half in range(2):
                        ins = mm3(e,
                                  PSMAP[pbase + 2 * oi][:, half * 512:(half + 1) * 512],
                                  wsb_hi(wslot, oi * 128, (oi + 1) * 128),
                                  wsb_lo(wslot, oi * 128, (oi + 1) * 128),
                                  hbuf[:, hsl * 2048 + half * 512:hsl * 2048 + (half + 1) * 512],
                                  hbuf[:, hsl * 2048 + 1024 + half * 512:hsl * 2048 + 1024 + (half + 1) * 512],
                                  first=(ich == 0), last=(ich == 63))
                return ins
            P("te", dmm, reads=[r_wsb[wslot], r_h[hsl]],
              writes=[rbank[pbase + i] for i in range(4)])
        for oi in range(2):
            ech = eg * 2 + oi
            sl = ech % 2
            xs = esl[:, sl * 1024:(sl + 1) * 1024]
            P("syncd", lambda e, ech=ech, xs=xs: e.dma_start(out=xs, in_=x2_scr[ech * 128:(ech + 1) * 128, :]),
              reads=[d_x2], writes=[r_esl[sl]], sem="de%ds" % sl)
            P("dve", lambda e, ech=ech, oi=oi, pbase=pbase, xs=xs: e.scalar_tensor_tensor(
                scrb[:, (oi % 2) * 1024:(oi % 2 + 1) * 1024],
                PSMAP[pbase + 2 * oi][:, 0:1024], gb_ap(ech), xs,
                op0=AL.mult, op1=AL.add),
              reads=[rbank[pbase + 2 * oi], rbank[pbase + 2 * oi + 1], r_esl[sl], r_modd],
              writes=[r_scr])
            P("gpsd", lambda e, ech=ech, oi=oi: e.dma_start(
                out=out_fm[ech * 128:(ech + 1) * 128, :],
                in_=scrb[:, (oi % 2) * 1024:(oi % 2 + 1) * 1024]),
              reads=[r_scr], writes=[d_out], sem="dout", waw=False)

    P("gps", lambda e: e.engine_nop(), reads=[d_out])

    sems = {}
    with contextlib.ExitStack() as st2:
        for s in plan.semnames:
            sems[s] = st2.enter_context(nc.semaphore("sem_" + s))
        with nc.Block() as block:
            block.tensor(plan.build_body(sems, "tensor"))
            block.vector(plan.build_body(sems, "vector"))
            block.scalar(plan.build_body(sems, "scalar"))
            block.gpsimd(plan.build_body(sems, "gpsimd"))
            block.sync(plan.build_body(sems, "sync"))
    stack.close()
    return nc


_NC_CACHE = [None]


def _host_tables():
    import jax
    import jax.numpy as jnp
    cpu = jax.devices("cpu")[0]
    with jax.default_device(cpu):
        inv_freq = 1.0 / 10000.0 ** (jnp.arange(0, D, 2, dtype=jnp.float32) / D)
        positions = jnp.arange(S, dtype=jnp.float32)
        freqs = jnp.outer(positions, inv_freq)
        emb = jnp.concatenate([freqs, freqs], axis=-1)
        cos = np.asarray(jnp.cos(emb), dtype=np.float32)
        sin = np.asarray(jnp.sin(emb), dtype=np.float32)
    return cos, sin


def kernel(L0_attn_x, L0_attn_Wqkv, L0_attn_Wout, L0_ffn_w_gate_up,
           L0_ffn_w_down, L0_c, L0_wmod):
    import ml_dtypes
    x = np.ascontiguousarray(np.asarray(L0_attn_x, dtype=np.float32))
    wqkvT = np.ascontiguousarray(np.asarray(L0_attn_Wqkv, np.float32).T)
    woutT = np.ascontiguousarray(np.asarray(L0_attn_Wout, np.float32).T)
    wguT = np.ascontiguousarray(np.asarray(L0_ffn_w_gate_up, np.float32).T)
    wdT = np.ascontiguousarray(np.asarray(L0_ffn_w_down, np.float32).T)
    cT = np.ascontiguousarray(np.asarray(L0_c, np.float32).T)
    wmod = np.ascontiguousarray(np.asarray(L0_wmod, np.float32))
    cos, sin = _host_tables()
    ident = np.eye(128, dtype=np.float32)
    ones1 = np.ones((1, 128), np.float32)
    onesc = np.ones((128, 1), np.float32)
    onesb = np.ones((128, 1), ml_dtypes.bfloat16)
    rotT = np.zeros((128, 128), np.float32)
    for i in range(64):
        rotT[2 * i + 1, 2 * i] = -1.0
        rotT[2 * i, 2 * i + 1] = 1.0
    negb = np.full((128, 1), SHIFT, np.float32)
    tri = (np.arange(128)[:, None] <= np.arange(128)[None, :]).astype(np.float32)

    if _NC_CACHE[0] is None:
        _NC_CACHE[0] = build_nc()
    nc = _NC_CACHE[0]

    in_maps = []
    for c in range(8):
        b, g = c // 4, c % 4
        order = slot_order(g)
        rows_idx = np.concatenate([np.arange(t * 128, (t + 1) * 128) for t in order])
        x_own = np.ascontiguousarray(x[b][rows_idx])
        cos_fm = np.ascontiguousarray(cos[rows_idx].T)
        sin_fm = np.ascontiguousarray(sin[rows_idx].T)
        masks = np.zeros((128, 8 * 4 * 128), ml_dtypes.bfloat16)
        for s in range(8):
            tg = order[s]
            C = 32 - 4 * s
            for j in range(4):
                kt = C - 4 + j
                if kt < tg:
                    m = np.ones((128, 128), ml_dtypes.bfloat16)
                elif kt == tg:
                    m = tri.astype(ml_dtypes.bfloat16)
                else:
                    m = np.zeros((128, 128), ml_dtypes.bfloat16)
                masks[:, (s * 4 + j) * 128:(s * 4 + j + 1) * 128] = m
        sel = np.zeros((128, 2), np.float32)
        sel[:, 0] = 1.0 - b
        sel[:, 1] = float(b)
        ksl = slice(c * 256, (c + 1) * 256)
        in_maps.append({
            "x_own": x_own, "wqkvT": wqkvT, "woutT": woutT, "wguT": wguT,
            "wdT": wdT, "wmod_sl": np.ascontiguousarray(wmod[ksl, :]),
            "ct_sl": np.ascontiguousarray(cT[ksl, :]),
            "cos_h": cos_fm, "sin_h": sin_fm, "ident_h": ident,
            "ones1_h": ones1, "onesc_h": onesc, "onesb_h": onesb,
            "rotT_h": rotT, "masks_h": masks, "negb_h": negb, "sel_h": sel,
        })

    res = run_bass_kernel_spmd(nc, in_maps, core_ids=list(range(8)))
    out = np.empty((2, 4096, 2048), np.float32)
    for c in range(8):
        b, g = c // 4, c % 4
        ofm = np.asarray(res.results[c]["out_fm"])
        otm = ofm.T
        for s, tg in enumerate(slot_order(g)):
            out[b, tg * 128:(tg + 1) * 128, :] = otm[s * 128:(s + 1) * 128, :]
    return out


# version 14
# speedup vs baseline: 1.0033x; 1.0033x over previous
import contextlib
import numpy as np
import concourse.bass as bass
import concourse.mybir as mybir
from concourse.bass_utils import run_bass_kernel_spmd

F32 = mybir.dt.float32
BF16 = mybir.dt.bfloat16
AX = mybir.AxisListType.X
AF = mybir.ActivationFunctionType
AL = mybir.AluOpType

B, S, E, H, D, INTER = 2, 4096, 2048, 16, 128, 8192
NT = 1024
NCH = E // 128
SCALE = D ** -0.5
SHIFT = -20.0

STREAMS = ["te", "dve", "act", "gps", "syncd", "actd", "gpsd"]
ENG_OF = {"te": "tensor", "dve": "vector", "act": "scalar", "gps": "gpsimd",
          "syncd": "sync", "actd": "scalar", "gpsd": "gpsimd"}


def slot_order(g):
    return [31 - g, 24 + g, 23 - g, 16 + g, 15 - g, 8 + g, 7 - g, g]


def kv_loc(t):
    m, r8 = t // 8, t % 8
    if r8 < 4:
        return r8, (7 - 2 * m) * 128
    return 7 - r8, (6 - 2 * m) * 128


class Res:
    __slots__ = ("w", "rs")

    def __init__(self):
        self.w = None
        self.rs = []


class Plan:
    def __init__(self):
        self.engines = sorted(set(ENG_OF.values()))
        self.steps = {e: [] for e in self.engines}
        self.count = {}
        self.semnames = []

    def step(self, stream, fn, reads=(), writes=(), extra=(), inc=None,
             sem=None, waw=True):
        if sem is None:
            sem = stream
        if sem not in self.count:
            self.count[sem] = 0
            self.semnames.append(sem)
        if inc is None:
            inc = 16 if stream.endswith("d") else 1
        waits = list(extra)
        for r in reads:
            if r.w is not None:
                waits.append(r.w)
        for r in writes:
            if waw and r.w is not None:
                waits.append(r.w)
            waits.extend(r.rs)
        w = {}
        for (s2, v2) in waits:
            if s2 == "te" and stream == "te":
                continue
            w[s2] = max(w.get(s2, 0), v2)
        self.steps[ENG_OF[stream]].append((sorted(w.items()), fn, sem, inc))
        self.count[sem] += inc
        tok = (sem, self.count[sem])
        for r in reads:
            r.rs.append(tok)
        for r in writes:
            r.w, r.rs = tok, []
        return tok

    def build_body(self, sems, engine):
        hiwater = {}

        def body(eng):
            for waits, fn, sem, inc in self.steps[engine]:
                for s2, v2 in waits:
                    if hiwater.get(s2, -1) >= v2:
                        continue
                    eng.wait_ge(sems[s2], v2)
                    hiwater[s2] = v2
                ins = fn(eng)
                ins.then_inc(sems[sem], inc)
        return body


def build_nc():
    nc = bass.Bass("TRN2", target_bir_lowering=False, debug=False)

    def dram(n, s, kind=None, dt=F32):
        t = nc.dram_tensor(n, s, dt, kind=kind) if kind else nc.dram_tensor(n, s, dt)
        return t.ap()
    x_own = dram("x_own", [NT, E], "ExternalInput")
    wqkvT = dram("wqkvT", [E, 3 * E], "ExternalInput")
    woutT = dram("woutT", [E, E], "ExternalInput")
    wguT = dram("wguT", [E, 2 * INTER], "ExternalInput")
    wdT = dram("wdT", [INTER, E], "ExternalInput")
    wmod_sl = dram("wmod_sl", [256, 6 * E], "ExternalInput")
    ct_sl = dram("ct_sl", [256, 2], "ExternalInput")
    cos_h = dram("cos_h", [128, NT], "ExternalInput")
    sin_h = dram("sin_h", [128, NT], "ExternalInput")
    ident_h = dram("ident_h", [128, 128], "ExternalInput")
    ones1_h = dram("ones1_h", [1, 128], "ExternalInput")
    onesc_h = dram("onesc_h", [128, 1], "ExternalInput")
    onesb_h = dram("onesb_h", [128, 1], "ExternalInput", BF16)
    rotT_h = dram("rotT_h", [128, 128], "ExternalInput")
    masks_h = dram("masks_h", [128, 8 * 4 * 128], "ExternalInput", BF16)
    negb_h = dram("negb_h", [128, 1], "ExternalInput")
    sel_h = dram("sel_h", [128, 2], "ExternalInput")
    xfm_scr = dram("xfm_scr", [E, NT])
    x2_scr = dram("x2_scr", [E, NT])
    kag_in = dram("kag_in", [2 * E, NT], dt=BF16)
    kag_out = dram("kag_out", [8 * E, NT], dt=BF16)
    vag_in = dram("vag_in", [2 * H * NT, 128], dt=BF16)
    vag_out = dram("vag_out", [8 * H * NT, 128], dt=BF16)
    mod_in = dram("mod_in", [128, 192])
    mod_out = dram("mod_out", [128, 192])
    h3_scr = dram("h3_scr", [2 * INTER, NT], dt=BF16)
    out_fm = dram("out_fm", [E, NT], "ExternalOutput")

    plan = Plan()
    P = plan.step
    groups_kv = [[0, 1, 2, 3], [4, 5, 6, 7]]
    group_all = [[0, 1, 2, 3, 4, 5, 6, 7]]

    stack = contextlib.ExitStack()

    def SB(n, sh, dt=F32):
        return stack.enter_context(nc.sbuf_tensor("s_" + n, sh, dt))

    PS = lambda n: stack.enter_context(nc.psum_tensor("p_" + n, [128, 1024], F32))

    ident = SB("ident", [128, 128]); ones1 = SB("ones1", [1, 128])
    onesc = SB("onesc", [128, 1]); rotT = SB("rotT", [128, 128])
    onesb = SB("onesb", [128, 1], BF16)
    cosb = SB("cos", [128, NT]); sinb = SB("sin", [128, NT])
    masks = SB("masks", [128, 4096], BF16)
    negb = SB("negb", [128, 1]); selb = SB("sel", [128, 2])
    ctsl = SB("ctsl", [128, 4]); silu_sl = SB("silu", [128, 4])
    modfm = SB("modfm", [128, 192]); ownmod = SB("ownmod", [128, 96])
    modd = SB("modd", [128, 64]); gwork = SB("gwork", [128, 32])
    hbuf = SB("hbuf", [128, 2 * NCH * NT], BF16)
    qbuf = SB("qbuf", [128, 2 * H * NT], BF16)
    wsl = SB("wsl", [128, 2048])
    wsb = SB("wsb", [128, 4096], BF16)
    kvb = SB("kvb", [128, 2816], BF16)
    esl = SB("esl", [128, 2048])
    eslb = SB("eslb", [128, 3072], BF16)
    h3sb = eslb
    ktb = eslb[:, 0:2048]
    vtb = eslb[:, 2048:3072]
    tmpa = SB("tmpa", [128, 2048])
    scrb = SB("scrb", [128, 2048])
    rows = SB("rows", [1, 2048])
    vt4 = scrb[:, 0:512]
    invb = tmpa[:, 0:1024]
    scol = SB("scol", [128, 8]); sqcol = SB("sqcol", [128, 8])
    modp = scrb[:, 1024:1216]

    psA, psB, psC, psZ = PS("A"), PS("B"), PS("C"), PS("Z")
    PSMAP = [psA, psA, psB, psB, psC, psC, psZ, psZ]

    rbank = [Res() for _ in range(8)]
    r_h = [Res() for _ in range(NCH)]
    r_q = [Res() for _ in range(H)]
    r_wsl = [Res() for _ in range(4)]
    r_wsb = [Res() for _ in range(4)]
    r_k = [Res() for _ in range(4)]; r_v = [Res() for _ in range(7)]
    r_esl = [Res(), Res()]
    r_eb = [Res(), Res()]
    r_tmpa = [Res(), Res()]
    r_scr = Res()
    r_rows = Res(); r_scol = Res(); r_sqcol = Res()
    r_consts = Res(); r_mod = Res(); r_own = Res(); r_modd = Res(); r_gw = Res()
    r_silu = Res()
    d_xfm = [Res(), Res()]; d_x2 = Res()
    d_kin = Res(); d_kout = Res(); d_vin = [Res(), Res()]; d_vout = Res()
    d_modin = Res(); d_modout = Res(); d_h3 = [Res(), Res()]; d_out = Res()

    def bank_ap(k, p0=0, p1=128, c0=0, c1=512):
        return PSMAP[k][p0:p1, (k % 2) * 512 + c0:(k % 2) * 512 + c1]

    def wsb_hi(sl, c0, c1):
        return wsb[:, sl * 1024 + c0:sl * 1024 + c1]

    def wsb_lo(sl, c0, c1):
        return wsb[:, sl * 1024 + 512 + c0:sl * 1024 + 512 + c1]

    def split_w(sl, width):
        P("dve", lambda e, sl=sl, width=width: e.tensor_copy(
            wsb_hi(sl, 0, width), wsl[:, sl * 512:sl * 512 + width]),
          reads=[r_wsl[sl]], writes=[r_wsb[sl]])
        P("dve", lambda e, sl=sl, width=width: e.tensor_sub(
            wsb_lo(sl, 0, width), wsl[:, sl * 512:sl * 512 + width], wsb_hi(sl, 0, width)),
          reads=[r_wsl[sl]], writes=[r_wsb[sl]])

    for dst, src in [(ident[:], ident_h[:]), (ones1[:], ones1_h[:]),
                     (onesc[:], onesc_h[:]), (onesb[:], onesb_h[:]),
                     (rotT[:], rotT_h[:]),
                     (cosb[:], cos_h[:]), (sinb[:], sin_h[:]),
                     (masks[:], masks_h[:]), (negb[:], negb_h[:]),
                     (selb[:], sel_h[:])]:
        P("syncd", lambda e, a=dst, b=src: e.dma_start(out=a, in_=b),
          writes=[r_consts], sem="dconst")
    for kc in range(2):
        P("gpsd", lambda e, kc=kc: e.dma_start(out=ctsl[:, kc * 2:kc * 2 + 2],
                                               in_=ct_sl[kc * 128:(kc + 1) * 128, :]),
          writes=[r_consts], sem="dct")

    P("act", lambda e: e.activation(silu_sl[:], ctsl[:], AF.Silu),
      reads=[r_consts], writes=[r_silu])
    for j in range(96):
        sl = j % 4
        P("syncd", lambda e, j=j, sl=sl: e.dma_start(
            out=wsl[:, sl * 512:sl * 512 + 256],
            in_=wmod_sl[:, j * 128:(j + 1) * 128].rearrange("(kc p) j -> p kc j", p=128)),
          writes=[r_wsl[sl]], sem="dw%d" % sl)

        def modmm(e, j=j, sl=sl):
            pt = psA if j < 48 else psB
            col = (j % 48) * 2
            for kc in range(2):
                ins = e.matmul(pt[:, col:col + 2],
                               wsl[:, sl * 512 + kc * 128:sl * 512 + (kc + 1) * 128],
                               silu_sl[:, kc * 2:kc * 2 + 2],
                               start=(kc == 0), stop=(kc == 1))
            return ins
        P("te", modmm, reads=[r_wsl[sl], r_silu], writes=[rbank[0 if j < 48 else 2]])
    P("dve", lambda e: e.tensor_copy(modp[:, 0:96], psA[:, 0:96]),
      reads=[rbank[0]], writes=[r_scr])
    P("dve", lambda e: e.tensor_copy(modp[:, 96:192], psB[:, 0:96]),
      reads=[rbank[2]], writes=[r_scr])
    P("gpsd", lambda e: e.dma_start(out=mod_in[:], in_=modp[:]),
      reads=[r_scr], writes=[d_modin], sem="dmodin")
    P("gps", lambda e: e.collective_compute(
        "AllReduce", AL.add, replica_groups=group_all,
        ins=[mod_in[:]], outs=[mod_out[:]]),
      reads=[d_modin], writes=[d_modout])
    P("gpsd", lambda e: e.dma_start(out=modfm[:], in_=mod_out[:]),
      reads=[d_modout], writes=[r_mod], sem="dmodld")
    P("dve", lambda e: e.tensor_scalar(scrb[:, 0:96], modfm[:, 0:192:2],
                                       selb[:, 0:1], None, op0=AL.mult),
      reads=[r_mod, r_consts], writes=[r_scr])
    P("dve", lambda e: e.scalar_tensor_tensor(ownmod[:], modfm[:, 1:192:2],
                                              selb[:, 1:2], scrb[:, 0:96],
                                              op0=AL.mult, op1=AL.add),
      reads=[r_mod, r_consts, r_scr], writes=[r_own])
    P("dve", lambda e: e.tensor_scalar_add(modd[:, 0:16], ownmod[:, 16:32], 1.0),
      reads=[r_own], writes=[r_modd])
    P("dve", lambda e: e.tensor_scalar_add(modd[:, 16:32], ownmod[:, 64:80], 1.0),
      reads=[r_own], writes=[r_modd])
    for qq, base in ((2, 32), (5, 48)):
        P("dve", lambda e, qq=qq: e.tensor_single_scalar(gwork[:, 0:16], ownmod[:, qq * 16:(qq + 1) * 16], 0.0, op=AL.is_ge),
          reads=[r_own], writes=[r_gw])
        P("dve", lambda e: e.tensor_scalar(gwork[:, 0:16], gwork[:, 0:16], 2.0, -1.0, op0=AL.mult, op1=AL.add),
          reads=[r_gw], writes=[r_gw])
        P("act", lambda e, qq=qq: e.activation(gwork[:, 16:32], ownmod[:, qq * 16:(qq + 1) * 16], AF.Abs),
          reads=[r_own], writes=[r_gw])
        P("dve", lambda e: e.tensor_scalar_max(gwork[:, 16:32], gwork[:, 16:32], 0.25),
          reads=[r_gw], writes=[r_gw])
        P("dve", lambda e, base=base: e.tensor_mul(modd[:, base:base + 16], gwork[:, 0:16], gwork[:, 16:32]),
          reads=[r_gw], writes=[r_modd])

    sa_ap = lambda ch: ownmod[:, ch:ch + 1]
    sb_ap = lambda ch: ownmod[:, 48 + ch:48 + ch + 1]
    ca1_ap = lambda ch: modd[:, ch:ch + 1]
    cb1_ap = lambda ch: modd[:, 16 + ch:16 + ch + 1]
    ga_ap = lambda ch: modd[:, 32 + ch:32 + ch + 1]
    gb_ap = lambda ch: modd[:, 48 + ch:48 + ch + 1]

    for til in range(8):
        xs = esl[:, 0:2048]
        P("actd", lambda e, til=til, xs=xs: e.dma_start(out=xs, in_=x_own[til * 128:(til + 1) * 128, :]),
          writes=[r_esl[0], r_esl[1]], sem="dx0")
        P("dve", lambda e, til=til, xs=xs: e.reduce_sum(scol[:, til:til + 1], xs, axis=AX),
          reads=[r_esl[0], r_esl[1]], writes=[r_scol])
        P("act", lambda e, til=til, xs=xs: e.activation(scrb[:], xs, AF.Square, accum_out=sqcol[:, til:til + 1]),
          reads=[r_esl[0], r_esl[1]], writes=[r_scr, r_sqcol])
        for bb in range(4):
            def trn(e, til=til, xs=xs, bb=bb):
                for k in range(4):
                    ch = bb * 4 + k
                    ins = e.transpose(bank_ap(bb % 4, c0=k * 128, c1=(k + 1) * 128),
                                      xs[:, ch * 128:(ch + 1) * 128], ident[:])
                return ins
            P("te", trn, reads=[r_esl[0], r_esl[1], r_consts], writes=[rbank[bb % 4]])
            P("dve", lambda e, bb=bb: e.tensor_copy(tmpa[:, (bb % 2) * 1024:(bb % 2) * 1024 + 512],
                                                    bank_ap(bb % 4)),
              reads=[rbank[bb % 4]], writes=[r_tmpa[bb % 2]])
            for k in range(4):
                ch = bb * 4 + k
                P("gpsd", lambda e, til=til, ch=ch, bb=bb, k=k: e.dma_start(
                    out=xfm_scr[ch * 128:(ch + 1) * 128, til * 128:(til + 1) * 128],
                    in_=tmpa[:, (bb % 2) * 1024 + k * 128:(bb % 2) * 1024 + (k + 1) * 128]),
                  reads=[r_tmpa[bb % 2]], writes=[d_xfm[bb % 2]], sem="dxfm%d" % (bb % 2), waw=False)
    for til in range(8):
        P("te", lambda e, til=til: e.transpose(bank_ap(4 + til // 4, p1=1, c0=(til % 4) * 128, c1=(til % 4 + 1) * 128),
                                               scol[:, til:til + 1], ident[:]),
          reads=[r_scol, r_consts], writes=[rbank[4 + til // 4]])
        P("te", lambda e, til=til: e.transpose(bank_ap(6 + til // 4, p1=1, c0=(til % 4) * 128, c1=(til % 4 + 1) * 128),
                                               sqcol[:, til:til + 1], ident[:]),
          reads=[r_sqcol, r_consts], writes=[rbank[6 + til // 4]])

    def stats_rows(sum_banks, sq_banks, sum_src, sq_src):
        P("act", lambda e: e.mul(rows[:, 0:1024], sum_src, 1.0 / 2048.0),
          reads=sum_banks, writes=[r_rows])
        P("act", lambda e: e.mul(rows[:, 1024:2048], sq_src, 1.0 / 2048.0),
          reads=sq_banks, writes=[r_rows])
        P("act", lambda e: e.activation(scrb[0:1, 0:1024], rows[:, 0:1024], AF.Square),
          reads=[r_rows], writes=[r_scr])
        P("dve", lambda e: e.tensor_sub(rows[:, 1024:2048], rows[:, 1024:2048], scrb[0:1, 0:1024]),
          reads=[r_scr], writes=[r_rows])
        P("dve", lambda e: e.tensor_scalar_add(rows[:, 1024:2048], rows[:, 1024:2048], 1e-5),
          reads=[r_rows], writes=[r_rows])
        P("act", lambda e: e.activation(scrb[0:1, 0:1024], rows[:, 1024:2048], AF.Sqrt),
          reads=[r_rows], writes=[r_scr])
        P("dve", lambda e: e.reciprocal(rows[:, 1024:2048], scrb[0:1, 0:1024]),
          reads=[r_scr], writes=[r_rows])

    stats_rows([rbank[4], rbank[5]], [rbank[6], rbank[7]],
               psC[0:1, 0:1024], psZ[0:1, 0:1024])

    def bcast_rows():
        for half in range(2):
            P("te", lambda e, half=half: e.matmul(
                psA[:, half * 512:(half + 1) * 512], ones1[:],
                rows[0:1, half * 512:(half + 1) * 512], start=True, stop=True),
              reads=[r_rows, r_consts], writes=[rbank[half]])
            P("te", lambda e, half=half: e.matmul(
                psB[:, half * 512:(half + 1) * 512], ones1[:],
                rows[0:1, 1024 + half * 512:1024 + (half + 1) * 512], start=True, stop=True),
              reads=[r_rows, r_consts], writes=[rbank[2 + half]])

    def ln_apply(c1ap, shap, dst_res, dst_hi, dst_lo, src_res, src_ap):
        P("dve", lambda e: e.tensor_sub(scrb[:, 0:1024], src_ap, psA[:, 0:1024]),
          reads=[src_res, rbank[0], rbank[1]], writes=[r_scr])
        P("dve", lambda e: e.tensor_mul(scrb[:, 1024:2048], scrb[:, 0:1024], psB[:, 0:1024]),
          reads=[r_scr, rbank[2], rbank[3]], writes=[r_scr])
        P("dve", lambda e: e.tensor_scalar(scrb[:, 0:1024], scrb[:, 1024:2048], c1ap, shap,
                                           op0=AL.mult, op1=AL.add),
          reads=[r_scr, r_modd, r_own], writes=[r_scr])
        P("dve", lambda e: e.tensor_copy(dst_hi, scrb[:, 0:1024]),
          reads=[r_scr], writes=[dst_res])
        P("dve", lambda e: e.tensor_sub(dst_lo, scrb[:, 0:1024], dst_hi),
          reads=[r_scr], writes=[dst_res])

    bcast_rows()
    for ch in range(NCH):
        sl = ch % 2
        xs = esl[:, sl * 1024:(sl + 1) * 1024]
        P("syncd", lambda e, ch=ch, xs=xs: e.dma_start(out=xs, in_=xfm_scr[ch * 128:(ch + 1) * 128, :]),
          reads=d_xfm, writes=[r_esl[sl]], sem="de%ds" % sl)
        ln_apply(ca1_ap(ch), sa_ap(ch), r_h[ch],
                 hbuf[:, ch * 2048:ch * 2048 + 1024],
                 hbuf[:, ch * 2048 + 1024:ch * 2048 + 2048],
                 r_esl[sl], xs)

    def mm3(e, out_ap, whi, wlo, xhi, xlo, first, last):
        e.matmul(out_ap, whi, xhi, start=first, stop=False)
        e.matmul(out_ap, whi, xlo, start=False, stop=False)
        ins = e.matmul(out_ap, wlo, xhi, start=False, stop=last)
        return ins

    def rope_chain(raw_res, raw_ap):
        for half in range(2):
            P("te", lambda e, half=half, raw_ap=raw_ap: e.matmul(
                bank_ap(6 + half), rotT[:], raw_ap[:, half * 512:(half + 1) * 512],
                start=True, stop=True),
              reads=[raw_res, r_consts], writes=[rbank[6 + half]])
        P("dve", lambda e: e.tensor_mul(scrb[:, 0:1024], psZ[:, 0:1024], sinb[:]),
          reads=[rbank[6], rbank[7], r_consts], writes=[r_scr])
        P("dve", lambda e, raw_ap=raw_ap: e.tensor_mul(scrb[:, 1024:2048], raw_ap, cosb[:]),
          reads=[raw_res, r_consts], writes=[r_scr])
        P("dve", lambda e: e.tensor_add(scrb[:, 0:1024], scrb[:, 0:1024], scrb[:, 1024:2048]),
          reads=[r_scr], writes=[r_scr])

    OG_ORDER = [11, 12, 13, 14, 15, 10, 9, 8, 7, 6, 5, 4, 3, 2, 1, 0]
    for og in OG_ORDER:
        ochs = [og * 3 + i for i in range(3)]
        for kch in range(NCH):
            sl = kch % 4
            P("syncd", lambda e, og=og, kch=kch, sl=sl: e.dma_start(
                out=wsl[:, sl * 512:sl * 512 + 384],
                in_=wqkvT[kch * 128:(kch + 1) * 128, og * 384:(og + 1) * 384]),
              writes=[r_wsl[sl]], sem="dw%d" % sl)
            split_w(sl, 384)

            def qkvmm(e, kch=kch, sl=sl):
                for oi in range(3):
                    for half in range(2):
                        ins = mm3(e,
                                  PSMAP[2 * oi][:, half * 512:(half + 1) * 512],
                                  wsb_hi(sl, oi * 128, (oi + 1) * 128),
                                  wsb_lo(sl, oi * 128, (oi + 1) * 128),
                                  hbuf[:, kch * 2048 + half * 512:kch * 2048 + (half + 1) * 512],
                                  hbuf[:, kch * 2048 + 1024 + half * 512:kch * 2048 + 1024 + (half + 1) * 512],
                                  first=(kch == 0), last=(kch == NCH - 1))
                return ins
            P("te", qkvmm, reads=[r_wsb[sl], r_h[kch]],
              writes=[rbank[0], rbank[1], rbank[2], rbank[3], rbank[4], rbank[5]])
        for oi, och in enumerate(ochs):
            asl = oi % 2
            raw = tmpa[:, asl * 1024:(asl + 1) * 1024]
            pX = PSMAP[2 * oi]
            P("dve", lambda e, pX=pX, raw=raw: e.tensor_copy(raw, pX[:, 0:1024]),
              reads=[rbank[2 * oi], rbank[2 * oi + 1]], writes=[r_tmpa[asl]])
            if och < 16:
                hh = och
                rope_chain(r_tmpa[asl], raw)
                P("dve", lambda e, hh=hh: e.tensor_copy(qbuf[:, hh * 2048:hh * 2048 + 1024], scrb[:, 0:1024]),
                  reads=[r_scr], writes=[r_q[hh]])
                P("dve", lambda e, hh=hh: e.tensor_sub(qbuf[:, hh * 2048 + 1024:hh * 2048 + 2048],
                                                       scrb[:, 0:1024], qbuf[:, hh * 2048:hh * 2048 + 1024]),
                  reads=[r_scr], writes=[r_q[hh]])
            elif och < 32:
                hh = och - 16
                rope_chain(r_tmpa[asl], raw)
                P("dve", lambda e: e.tensor_copy(ktb[:, 0:1024], scrb[:, 0:1024]),
                  reads=[r_scr], writes=[r_eb[0]])
                P("dve", lambda e: e.tensor_sub(ktb[:, 1024:2048], scrb[:, 0:1024], ktb[:, 0:1024]),
                  reads=[r_scr], writes=[r_eb[0]])
                P("gpsd", lambda e, hh=hh: e.dma_start(
                    out=kag_in[hh * 128:(hh + 1) * 128, :], in_=ktb[:, 0:1024]),
                  reads=[r_eb[0]], writes=[d_kin], sem="dkin", waw=False)
                P("gpsd", lambda e, hh=hh: e.dma_start(
                    out=kag_in[E + hh * 128:E + (hh + 1) * 128, :], in_=ktb[:, 1024:2048]),
                  reads=[r_eb[0]], writes=[d_kin], sem="dkin", waw=False)
            else:
                hh = och - 32
                for grp in range(2):
                    def vtr(e, raw=raw, grp=grp):
                        for k in range(4):
                            ins = e.transpose(bank_ap(6 + grp, c0=k * 128, c1=(k + 1) * 128),
                                              raw[:, (grp * 4 + k) * 128:(grp * 4 + k + 1) * 128],
                                              ident[:])
                        return ins
                    P("te", vtr, reads=[r_tmpa[asl], r_consts], writes=[rbank[6 + grp]])
                    P("dve", lambda e, grp=grp: e.tensor_copy(vt4[:], bank_ap(6 + grp)),
                      reads=[rbank[6 + grp]], writes=[r_scr])
                    P("dve", lambda e: e.tensor_copy(vtb[:, 0:512], vt4[:]),
                      reads=[r_scr], writes=[r_eb[1]])
                    P("dve", lambda e: e.tensor_sub(vtb[:, 512:1024], vt4[:], vtb[:, 0:512]),
                      reads=[r_scr], writes=[r_eb[1]])
                    for k in range(4):
                        sub = grp * 4 + k
                        P("gpsd", lambda e, hh=hh, sub=sub, k=k: e.dma_start(
                            out=vag_in[hh * NT + sub * 128:hh * NT + (sub + 1) * 128, :],
                            in_=vtb[:, k * 128:(k + 1) * 128]),
                          reads=[r_eb[1]], writes=[d_vin[grp]], sem="dvin%d" % grp, waw=False)
                        P("gpsd", lambda e, hh=hh, sub=sub, k=k: e.dma_start(
                            out=vag_in[H * NT + hh * NT + sub * 128:H * NT + hh * NT + (sub + 1) * 128, :],
                            in_=vtb[:, 512 + k * 128:512 + (k + 1) * 128]),
                          reads=[r_eb[1]], writes=[d_vin[grp]], sem="dvin%d" % grp, waw=False)
        if og == 10:
            P("gps", lambda e: e.collective_compute(
                "AllGather", AL.bypass, replica_groups=groups_kv,
                ins=[vag_in[:]], outs=[vag_out[:]]),
              reads=d_vin, writes=[d_vout])
        if og == 5:
            P("gps", lambda e: e.collective_compute(
                "AllGather", AL.bypass, replica_groups=groups_kv,
                ins=[kag_in[:]], outs=[kag_out[:]]),
              reads=[d_kin], writes=[d_kout])

    KBLK, VBLK = 2 * E, 2 * H * NT

    def eb_hi(sl3, c0, c1):
        return hbuf[:, sl3 * 2048 + c0:sl3 * 2048 + c1]

    def eb_lo(sl3, c0, c1):
        return hbuf[:, sl3 * 2048 + 1024 + c0:sl3 * 2048 + 1024 + c1]

    for hh in range(H):
        def front(kt, hh=hh):
            w = 128 * (8 - kt // 4)
            par = kt % 2
            sl3 = kt % 5
            ksl, vsl = kt % 4, kt % 7
            rank, col0 = kv_loc(kt)
            P("syncd", lambda e, hh=hh, rank=rank, col0=col0, ksl=ksl: e.dma_start(
                out=kvb[:, ksl * 256:ksl * 256 + 128],
                in_=kag_out[rank * KBLK + hh * 128:rank * KBLK + (hh + 1) * 128, col0:col0 + 128]),
              reads=[d_kout], writes=[r_k[ksl]], sem="dk%d" % ksl)
            P("syncd", lambda e, hh=hh, rank=rank, col0=col0, ksl=ksl: e.dma_start(
                out=kvb[:, ksl * 256 + 128:ksl * 256 + 256],
                in_=kag_out[rank * KBLK + E + hh * 128:rank * KBLK + E + (hh + 1) * 128, col0:col0 + 128]),
              reads=[d_kout], writes=[r_k[ksl]], sem="dk%d" % ksl)
            P("gpsd", lambda e, hh=hh, rank=rank, col0=col0, vsl=vsl: e.dma_start(
                out=kvb[:, 1024 + vsl * 256:1024 + vsl * 256 + 128],
                in_=vag_out[rank * VBLK + hh * NT + col0:rank * VBLK + hh * NT + col0 + 128, :]),
              reads=[d_vout], writes=[r_v[vsl]], sem="dv%d" % vsl)
            P("gpsd", lambda e, hh=hh, rank=rank, col0=col0, vsl=vsl: e.dma_start(
                out=kvb[:, 1024 + vsl * 256 + 128:1024 + vsl * 256 + 256],
                in_=vag_out[rank * VBLK + H * NT + hh * NT + col0:rank * VBLK + H * NT + hh * NT + col0 + 128, :]),
              reads=[d_vout], writes=[r_v[vsl]], sem="dv%d" % vsl)
            scp = psA if par == 0 else psB
            scb = [rbank[0], rbank[1]] if par == 0 else [rbank[2], rbank[3]]
            nh = 2 if w > 512 else 1

            def score(e, hh=hh, kt=kt, w=w, scp=scp, ksl=ksl):
                for half in range(2 if w > 512 else 1):
                    ww = min(512, w - half * 512)
                    ins = mm3(e, scp[:, half * 512:half * 512 + ww],
                              kvb[:, ksl * 256:ksl * 256 + 128],
                              kvb[:, ksl * 256 + 128:ksl * 256 + 256],
                              qbuf[:, hh * 2048 + half * 512:hh * 2048 + half * 512 + ww],
                              qbuf[:, hh * 2048 + 1024 + half * 512:hh * 2048 + 1024 + half * 512 + ww],
                              first=True, last=True)
                return ins
            P("te", score, reads=[r_k[ksl], r_q[hh]], writes=scb[:nh])
            P("act", lambda e, par=par, w=w, scp=scp: e.activation(
                esl[:, par * 1024:par * 1024 + w], scp[:, 0:w], AF.Exp,
                bias=negb[:, 0:1], scale=SCALE),
              reads=scb[:nh] + [r_consts], writes=[r_esl[par]])
            s_m, j_m = 7 - kt // 4, kt % 4
            P("dve", lambda e, par=par, s_m=s_m, j_m=j_m: e.tensor_mul(
                esl[:, par * 1024 + s_m * 128:par * 1024 + (s_m + 1) * 128],
                esl[:, par * 1024 + s_m * 128:par * 1024 + (s_m + 1) * 128],
                masks[:, (s_m * 4 + j_m) * 128:(s_m * 4 + j_m + 1) * 128]),
              reads=[r_consts], writes=[r_esl[par]])
            P("dve", lambda e, par=par, w=w, sl3=sl3: e.tensor_copy(
                eb_hi(sl3, 0, w), esl[:, par * 1024:par * 1024 + w]),
              reads=[r_esl[par]], writes=[r_h[sl3]])
            P("dve", lambda e, par=par, w=w, sl3=sl3: e.tensor_sub(
                eb_lo(sl3, 0, w), esl[:, par * 1024:par * 1024 + w], eb_hi(sl3, 0, w)),
              reads=[r_esl[par]], writes=[r_h[sl3]])

        def back(kt, hh=hh):
            w = 128 * (8 - kt // 4)
            sl3 = kt % 5
            vsl = kt % 7

            def colsum(e, kt=kt, w=w, sl3=sl3):
                for half in range(2 if w > 512 else 1):
                    ww = min(512, w - half * 512)
                    last = (kt == 31 if half == 0 else kt == 15)
                    e.matmul(psZ[0:1, half * 512:half * 512 + ww], onesb[:],
                             eb_hi(sl3, half * 512, half * 512 + ww),
                             start=(kt == 0), stop=False)
                    ins = e.matmul(psZ[0:1, half * 512:half * 512 + ww], onesb[:],
                                   eb_lo(sl3, half * 512, half * 512 + ww),
                                   start=False, stop=last)
                return ins
            P("te", colsum, reads=[r_h[sl3], r_consts], writes=[rbank[6]] + ([rbank[7]] if w > 512 else []))

            def avmm(e, kt=kt, w=w, sl3=sl3, vsl=vsl):
                for half in range(2 if w > 512 else 1):
                    ww = min(512, w - half * 512)
                    last = (kt == 31 if half == 0 else kt == 15)
                    ins = mm3(e, psC[:, half * 512:half * 512 + ww],
                              kvb[:, 1024 + vsl * 256:1024 + vsl * 256 + 128],
                              kvb[:, 1024 + vsl * 256 + 128:1024 + vsl * 256 + 256],
                              eb_hi(sl3, half * 512, half * 512 + ww),
                              eb_lo(sl3, half * 512, half * 512 + ww),
                              first=(kt == 0), last=last)
                return ins
            P("te", avmm, reads=[r_h[sl3], r_v[vsl]], writes=[rbank[4]] + ([rbank[5]] if w > 512 else []))

        for kt in range(36):
            if kt < 32:
                front(kt)
            if kt >= 4:
                back(kt - 4)
        P("dve", lambda e: e.reciprocal(scrb[0:1, 0:1024], psZ[0:1, 0:1024]),
          reads=[rbank[6], rbank[7]], writes=[r_scr])
        for half in range(2):
            P("te", lambda e, half=half: e.matmul(
                psA[:, half * 512:(half + 1) * 512], ones1[:],
                scrb[0:1, half * 512:(half + 1) * 512], start=True, stop=True),
              reads=[r_scr, r_consts], writes=[rbank[half]])
        P("dve", lambda e: e.tensor_copy(invb[:], psA[:, 0:1024]),
          reads=[rbank[0], rbank[1]], writes=[r_tmpa[0]])
        P("dve", lambda e: e.tensor_mul(tmpa[:, 1024:2048], psC[:, 0:1024], invb[:]),
          reads=[rbank[4], rbank[5], r_tmpa[0]], writes=[r_tmpa[1]])
        P("dve", lambda e, hh=hh: e.tensor_copy(qbuf[:, hh * 2048:hh * 2048 + 1024], tmpa[:, 1024:2048]),
          reads=[r_tmpa[1]], writes=[r_q[hh]])
        P("dve", lambda e, hh=hh: e.tensor_sub(qbuf[:, hh * 2048 + 1024:hh * 2048 + 2048],
                                               tmpa[:, 1024:2048], qbuf[:, hh * 2048:hh * 2048 + 1024]),
          reads=[r_tmpa[1]], writes=[r_q[hh]])

    for og in range(8):
        pbase = 0 if og % 2 == 0 else 4
        for kch in range(NCH):
            sl = kch % 4
            P("syncd", lambda e, og=og, kch=kch, sl=sl: e.dma_start(
                out=wsl[:, sl * 512:sl * 512 + 256],
                in_=woutT[kch * 128:(kch + 1) * 128, og * 256:(og + 1) * 256]),
              writes=[r_wsl[sl]], sem="dw%d" % sl)
            split_w(sl, 256)

            def womm(e, kch=kch, sl=sl, pbase=pbase):
                for oi in range(2):
                    for half in range(2):
                        ins = mm3(e,
                                  PSMAP[pbase + 2 * oi][:, half * 512:(half + 1) * 512],
                                  wsb_hi(sl, oi * 128, (oi + 1) * 128),
                                  wsb_lo(sl, oi * 128, (oi + 1) * 128),
                                  qbuf[:, kch * 2048 + half * 512:kch * 2048 + (half + 1) * 512],
                                  qbuf[:, kch * 2048 + 1024 + half * 512:kch * 2048 + 1024 + (half + 1) * 512],
                                  first=(kch == 0), last=(kch == NCH - 1))
                return ins
            P("te", womm, reads=[r_wsb[sl], r_q[kch]],
              writes=[rbank[pbase + i] for i in range(4)])
        for oi in range(2):
            ech = og * 2 + oi
            sl = ech % 2
            xs = esl[:, sl * 1024:(sl + 1) * 1024]
            P("syncd", lambda e, ech=ech, xs=xs: e.dma_start(out=xs, in_=xfm_scr[ech * 128:(ech + 1) * 128, :]),
              reads=d_xfm, writes=[r_esl[sl]], sem="de%ds" % sl)
            P("dve", lambda e, ech=ech, oi=oi, pbase=pbase, xs=xs: e.scalar_tensor_tensor(
                scrb[:, 0:1024],
                PSMAP[pbase + 2 * oi][:, 0:1024], ga_ap(ech), xs,
                op0=AL.mult, op1=AL.add),
              reads=[rbank[pbase + 2 * oi], rbank[pbase + 2 * oi + 1], r_esl[sl], r_modd],
              writes=[r_scr])
            P("gpsd", lambda e, ech=ech: e.dma_start(
                out=x2_scr[ech * 128:(ech + 1) * 128, :], in_=scrb[:, 0:1024]),
              reads=[r_scr], writes=[d_x2], sem="dx2", waw=False)

    for ch in range(NCH):
        sl = ch % 2
        xs = esl[:, sl * 1024:(sl + 1) * 1024]
        P("syncd", lambda e, ch=ch, xs=xs: e.dma_start(out=xs, in_=x2_scr[ch * 128:(ch + 1) * 128, :]),
          reads=[d_x2], writes=[r_esl[sl]], sem="de%ds" % sl)
        P("te", lambda e, ch=ch, xs=xs: e.matmul(psZ[0:1, 0:512], onesc[:], xs[:, 0:512],
                                                 start=(ch == 0), stop=(ch == NCH - 1)),
          reads=[r_esl[sl], r_consts], writes=[rbank[6]])
        P("te", lambda e, ch=ch, xs=xs: e.matmul(psZ[0:1, 512:1024], onesc[:], xs[:, 512:1024],
                                                 start=(ch == 0), stop=(ch == NCH - 1)),
          reads=[r_esl[sl], r_consts], writes=[rbank[7]])
        P("act", lambda e, ch=ch, xs=xs: e.activation(scrb[:, 0:1024], xs, AF.Square),
          reads=[r_esl[sl]], writes=[r_scr])
        P("te", lambda e, ch=ch: e.matmul(psC[0:1, 0:512], onesc[:], scrb[:, 0:512],
                                          start=(ch == 0), stop=(ch == NCH - 1)),
          reads=[r_scr, r_consts], writes=[rbank[4]])
        P("te", lambda e, ch=ch: e.matmul(psC[0:1, 512:1024], onesc[:], scrb[:, 512:1024],
                                          start=(ch == 0), stop=(ch == NCH - 1)),
          reads=[r_scr, r_consts], writes=[rbank[5]])
    stats_rows([rbank[6], rbank[7]], [rbank[4], rbank[5]],
               psZ[0:1, 0:1024], psC[0:1, 0:1024])
    bcast_rows()
    for ch in range(NCH):
        sl = ch % 2
        xs = esl[:, sl * 1024:(sl + 1) * 1024]
        P("syncd", lambda e, ch=ch, xs=xs: e.dma_start(out=xs, in_=x2_scr[ch * 128:(ch + 1) * 128, :]),
          reads=[d_x2], writes=[r_esl[sl]], sem="de%ds" % sl)
        ln_apply(cb1_ap(ch), sb_ap(ch), r_h[ch],
                 hbuf[:, ch * 2048:ch * 2048 + 1024],
                 hbuf[:, ch * 2048 + 1024:ch * 2048 + 2048],
                 r_esl[sl], xs)

    for i in range(64):
        par = i % 2
        gb_, ub_ = (0, 2) if par == 0 else (4, 6)
        for kch in range(NCH):
            sl = (i * NCH + kch) % 4
            P("syncd", lambda e, i=i, kch=kch, sl=sl: e.dma_start(
                out=wsl[:, sl * 512:sl * 512 + 128],
                in_=wguT[kch * 128:(kch + 1) * 128, i * 128:(i + 1) * 128]),
              writes=[r_wsl[sl]], sem="dw%d" % sl)
            P("syncd", lambda e, i=i, kch=kch, sl=sl: e.dma_start(
                out=wsl[:, sl * 512 + 128:sl * 512 + 256],
                in_=wguT[kch * 128:(kch + 1) * 128, INTER + i * 128:INTER + (i + 1) * 128]),
              writes=[r_wsl[sl]], sem="dw%d" % sl)
            split_w(sl, 256)

            def gumm(e, kch=kch, sl=sl, gb_=gb_, ub_=ub_):
                for half in range(2):
                    xhi = hbuf[:, kch * 2048 + half * 512:kch * 2048 + (half + 1) * 512]
                    xlo = hbuf[:, kch * 2048 + 1024 + half * 512:kch * 2048 + 1024 + (half + 1) * 512]
                    mm3(e, PSMAP[gb_][:, half * 512:(half + 1) * 512],
                        wsb_hi(sl, 0, 128), wsb_lo(sl, 0, 128), xhi, xlo,
                        first=(kch == 0), last=(kch == NCH - 1))
                    ins = mm3(e, PSMAP[ub_][:, half * 512:(half + 1) * 512],
                              wsb_hi(sl, 128, 256), wsb_lo(sl, 128, 256), xhi, xlo,
                              first=(kch == 0), last=(kch == NCH - 1))
                return ins
            P("te", gumm, reads=[r_wsb[sl], r_h[kch]],
              writes=[rbank[gb_], rbank[gb_ + 1], rbank[ub_], rbank[ub_ + 1]])
        sil = tmpa[:, par * 1024:(par + 1) * 1024]
        P("act", lambda e, gb_=gb_, sil=sil: e.activation(sil, PSMAP[gb_][:, 0:1024], AF.Silu),
          reads=[rbank[gb_], rbank[gb_ + 1]], writes=[r_tmpa[par]])
        h3ap = scrb[:, par * 1024:(par + 1) * 1024]
        P("dve", lambda e, ub_=ub_, sil=sil, h3ap=h3ap: e.tensor_mul(h3ap, sil, PSMAP[ub_][:, 0:1024]),
          reads=[r_tmpa[par], rbank[ub_], rbank[ub_ + 1]], writes=[r_scr])
        P("dve", lambda e, par=par, h3ap=h3ap: e.tensor_copy(qbuf[:, par * 2048:par * 2048 + 1024], h3ap),
          reads=[r_scr], writes=[r_q[par]])
        P("dve", lambda e, par=par, h3ap=h3ap: e.tensor_sub(
            qbuf[:, par * 2048 + 1024:par * 2048 + 2048], h3ap, qbuf[:, par * 2048:par * 2048 + 1024]),
          reads=[r_scr], writes=[r_q[par]])
        P("gpsd", lambda e, i=i, par=par: e.dma_start(
            out=h3_scr[i * 128:(i + 1) * 128, :], in_=qbuf[:, par * 2048:par * 2048 + 1024]),
          reads=[r_q[par]], writes=[d_h3[par]], sem="dh3%d" % par, waw=False)
        P("gpsd", lambda e, i=i, par=par: e.dma_start(
            out=h3_scr[INTER + i * 128:INTER + (i + 1) * 128, :],
            in_=qbuf[:, par * 2048 + 1024:par * 2048 + 2048]),
          reads=[r_q[par]], writes=[d_h3[par]], sem="dh3%d" % par, waw=False)
    for eg in range(8):
        pbase = 0 if eg % 2 == 0 else 4
        for ich in range(64):
            wslot = ich % 4
            P("syncd", lambda e, eg=eg, ich=ich, wslot=wslot: e.dma_start(
                out=wsl[:, wslot * 512:wslot * 512 + 256],
                in_=wdT[ich * 128:(ich + 1) * 128, eg * 256:(eg + 1) * 256]),
              writes=[r_wsl[wslot]], sem="dw%d" % wslot)
            split_w(wslot, 256)
            hsl = ich % 8
            P("actd", lambda e, ich=ich, hsl=hsl: e.dma_start(
                out=hbuf[:, hsl * 2048:hsl * 2048 + 1024],
                in_=h3_scr[ich * 128:(ich + 1) * 128, :]),
              reads=d_h3, writes=[r_h[hsl]], sem="de%da" % hsl)
            P("actd", lambda e, ich=ich, hsl=hsl: e.dma_start(
                out=hbuf[:, hsl * 2048 + 1024:hsl * 2048 + 2048],
                in_=h3_scr[INTER + ich * 128:INTER + (ich + 1) * 128, :]),
              reads=d_h3, writes=[r_h[hsl]], sem="de%da" % hsl)

            def dmm(e, ich=ich, wslot=wslot, pbase=pbase, hsl=hsl):
                for oi in range(2):
                    for half in range(2):
                        ins = mm3(e,
                                  PSMAP[pbase + 2 * oi][:, half * 512:(half + 1) * 512],
                                  wsb_hi(wslot, oi * 128, (oi + 1) * 128),
                                  wsb_lo(wslot, oi * 128, (oi + 1) * 128),
                                  hbuf[:, hsl * 2048 + half * 512:hsl * 2048 + (half + 1) * 512],
                                  hbuf[:, hsl * 2048 + 1024 + half * 512:hsl * 2048 + 1024 + (half + 1) * 512],
                                  first=(ich == 0), last=(ich == 63))
                return ins
            P("te", dmm, reads=[r_wsb[wslot], r_h[hsl]],
              writes=[rbank[pbase + i] for i in range(4)])
        for oi in range(2):
            ech = eg * 2 + oi
            sl = ech % 2
            xs = esl[:, sl * 1024:(sl + 1) * 1024]
            P("syncd", lambda e, ech=ech, xs=xs: e.dma_start(out=xs, in_=x2_scr[ech * 128:(ech + 1) * 128, :]),
              reads=[d_x2], writes=[r_esl[sl]], sem="de%ds" % sl)
            P("dve", lambda e, ech=ech, oi=oi, pbase=pbase, xs=xs: e.scalar_tensor_tensor(
                scrb[:, (oi % 2) * 1024:(oi % 2 + 1) * 1024],
                PSMAP[pbase + 2 * oi][:, 0:1024], gb_ap(ech), xs,
                op0=AL.mult, op1=AL.add),
              reads=[rbank[pbase + 2 * oi], rbank[pbase + 2 * oi + 1], r_esl[sl], r_modd],
              writes=[r_scr])
            P("gpsd", lambda e, ech=ech, oi=oi: e.dma_start(
                out=out_fm[ech * 128:(ech + 1) * 128, :],
                in_=scrb[:, (oi % 2) * 1024:(oi % 2 + 1) * 1024]),
              reads=[r_scr], writes=[d_out], sem="dout", waw=False)

    P("gps", lambda e: e.engine_nop(), reads=[d_out])

    sems = {}
    with contextlib.ExitStack() as st2:
        for s in plan.semnames:
            sems[s] = st2.enter_context(nc.semaphore("sem_" + s))
        with nc.Block() as block:
            block.tensor(plan.build_body(sems, "tensor"))
            block.vector(plan.build_body(sems, "vector"))
            block.scalar(plan.build_body(sems, "scalar"))
            block.gpsimd(plan.build_body(sems, "gpsimd"))
            block.sync(plan.build_body(sems, "sync"))
    stack.close()
    return nc


_NC_CACHE = [None]


def _host_tables():
    import jax
    import jax.numpy as jnp
    cpu = jax.devices("cpu")[0]
    with jax.default_device(cpu):
        inv_freq = 1.0 / 10000.0 ** (jnp.arange(0, D, 2, dtype=jnp.float32) / D)
        positions = jnp.arange(S, dtype=jnp.float32)
        freqs = jnp.outer(positions, inv_freq)
        emb = jnp.concatenate([freqs, freqs], axis=-1)
        cos = np.asarray(jnp.cos(emb), dtype=np.float32)
        sin = np.asarray(jnp.sin(emb), dtype=np.float32)
    return cos, sin


def kernel(L0_attn_x, L0_attn_Wqkv, L0_attn_Wout, L0_ffn_w_gate_up,
           L0_ffn_w_down, L0_c, L0_wmod):
    import ml_dtypes
    x = np.ascontiguousarray(np.asarray(L0_attn_x, dtype=np.float32))
    wqkvT = np.ascontiguousarray(np.asarray(L0_attn_Wqkv, np.float32).T)
    woutT = np.ascontiguousarray(np.asarray(L0_attn_Wout, np.float32).T)
    wguT = np.ascontiguousarray(np.asarray(L0_ffn_w_gate_up, np.float32).T)
    wdT = np.ascontiguousarray(np.asarray(L0_ffn_w_down, np.float32).T)
    cT = np.ascontiguousarray(np.asarray(L0_c, np.float32).T)
    wmod = np.ascontiguousarray(np.asarray(L0_wmod, np.float32))
    cos, sin = _host_tables()
    ident = np.eye(128, dtype=np.float32)
    ones1 = np.ones((1, 128), np.float32)
    onesc = np.ones((128, 1), np.float32)
    onesb = np.ones((128, 1), ml_dtypes.bfloat16)
    rotT = np.zeros((128, 128), np.float32)
    for i in range(64):
        rotT[2 * i + 1, 2 * i] = -1.0
        rotT[2 * i, 2 * i + 1] = 1.0
    negb = np.full((128, 1), SHIFT, np.float32)
    tri = (np.arange(128)[:, None] <= np.arange(128)[None, :]).astype(np.float32)

    if _NC_CACHE[0] is None:
        _NC_CACHE[0] = build_nc()
    nc = _NC_CACHE[0]

    in_maps = []
    for c in range(8):
        b, g = c // 4, c % 4
        order = slot_order(g)
        rows_idx = np.concatenate([np.arange(t * 128, (t + 1) * 128) for t in order])
        x_own = np.ascontiguousarray(x[b][rows_idx])
        cos_fm = np.ascontiguousarray(cos[rows_idx].T)
        sin_fm = np.ascontiguousarray(sin[rows_idx].T)
        masks = np.zeros((128, 8 * 4 * 128), ml_dtypes.bfloat16)
        for s in range(8):
            tg = order[s]
            C = 32 - 4 * s
            for j in range(4):
                kt = C - 4 + j
                if kt < tg:
                    m = np.ones((128, 128), ml_dtypes.bfloat16)
                elif kt == tg:
                    m = tri.astype(ml_dtypes.bfloat16)
                else:
                    m = np.zeros((128, 128), ml_dtypes.bfloat16)
                masks[:, (s * 4 + j) * 128:(s * 4 + j + 1) * 128] = m
        sel = np.zeros((128, 2), np.float32)
        sel[:, 0] = 1.0 - b
        sel[:, 1] = float(b)
        ksl = slice(c * 256, (c + 1) * 256)
        in_maps.append({
            "x_own": x_own, "wqkvT": wqkvT, "woutT": woutT, "wguT": wguT,
            "wdT": wdT, "wmod_sl": np.ascontiguousarray(wmod[ksl, :]),
            "ct_sl": np.ascontiguousarray(cT[ksl, :]),
            "cos_h": cos_fm, "sin_h": sin_fm, "ident_h": ident,
            "ones1_h": ones1, "onesc_h": onesc, "onesb_h": onesb,
            "rotT_h": rotT, "masks_h": masks, "negb_h": negb, "sel_h": sel,
        })

    res = run_bass_kernel_spmd(nc, in_maps, core_ids=list(range(8)))
    out = np.empty((2, 4096, 2048), np.float32)
    for c in range(8):
        b, g = c // 4, c % 4
        ofm = np.asarray(res.results[c]["out_fm"])
        otm = ofm.T
        for s, tg in enumerate(slot_order(g)):
            out[b, tg * 128:(tg + 1) * 128, :] = otm[s * 128:(s + 1) * 128, :]
    return out


# version 15 (final)
# speedup vs baseline: 1.0480x; 1.0446x over previous
import contextlib
import numpy as np
import concourse.bass as bass
import concourse.mybir as mybir
from concourse.bass_utils import run_bass_kernel_spmd

F32 = mybir.dt.float32
BF16 = mybir.dt.bfloat16
AX = mybir.AxisListType.X
AF = mybir.ActivationFunctionType
AL = mybir.AluOpType

B, S, E, H, D, INTER = 2, 4096, 2048, 16, 128, 8192
NT = 1024
NCH = E // 128
SCALE = D ** -0.5
SHIFT = -20.0

STREAMS = ["te", "dve", "act", "gps", "syncd", "actd", "gpsd"]
ENG_OF = {"te": "tensor", "dve": "vector", "act": "scalar", "gps": "gpsimd",
          "syncd": "sync", "actd": "scalar", "gpsd": "gpsimd"}


def slot_order(g):
    return [31 - g, 24 + g, 23 - g, 16 + g, 15 - g, 8 + g, 7 - g, g]


def kv_loc(t):
    m, r8 = t // 8, t % 8
    if r8 < 4:
        return r8, (7 - 2 * m) * 128
    return 7 - r8, (6 - 2 * m) * 128


class Res:
    __slots__ = ("w", "rs")

    def __init__(self):
        self.w = None
        self.rs = []


class Plan:
    def __init__(self):
        self.engines = sorted(set(ENG_OF.values()))
        self.steps = {e: [] for e in self.engines}
        self.count = {}
        self.semnames = []

    def step(self, stream, fn, reads=(), writes=(), extra=(), inc=None,
             sem=None, waw=True):
        if sem is None:
            sem = stream
        if sem not in self.count:
            self.count[sem] = 0
            self.semnames.append(sem)
        if inc is None:
            inc = 16 if stream.endswith("d") else 1
        waits = list(extra)
        for r in reads:
            if r.w is not None:
                waits.append(r.w)
        for r in writes:
            if waw and r.w is not None:
                waits.append(r.w)
            waits.extend(r.rs)
        w = {}
        for (s2, v2) in waits:
            if s2 == "te" and stream == "te":
                continue
            w[s2] = max(w.get(s2, 0), v2)
        self.steps[ENG_OF[stream]].append((sorted(w.items()), fn, sem, inc))
        self.count[sem] += inc
        tok = (sem, self.count[sem])
        for r in reads:
            r.rs.append(tok)
        for r in writes:
            r.w, r.rs = tok, []
        return tok

    def build_body(self, sems, engine):
        hiwater = {}

        def body(eng):
            for waits, fn, sem, inc in self.steps[engine]:
                for s2, v2 in waits:
                    if hiwater.get(s2, -1) >= v2:
                        continue
                    eng.wait_ge(sems[s2], v2)
                    hiwater[s2] = v2
                ins = fn(eng)
                ins.then_inc(sems[sem], inc)
        return body


def build_nc():
    nc = bass.Bass("TRN2", target_bir_lowering=False, debug=False)

    def dram(n, s, kind=None, dt=F32):
        t = nc.dram_tensor(n, s, dt, kind=kind) if kind else nc.dram_tensor(n, s, dt)
        return t.ap()
    x_own = dram("x_own", [NT, E], "ExternalInput")
    wqkvT = dram("wqkvT", [E, 3 * E], "ExternalInput")
    woutT = dram("woutT", [E, E], "ExternalInput")
    wguT = dram("wguT", [E, 2 * INTER], "ExternalInput")
    wdT = dram("wdT", [INTER, E], "ExternalInput")
    wmod_sl = dram("wmod_sl", [256, 6 * E], "ExternalInput")
    ct_sl = dram("ct_sl", [256, 2], "ExternalInput")
    cos_h = dram("cos_h", [128, NT], "ExternalInput")
    sin_h = dram("sin_h", [128, NT], "ExternalInput")
    ident_h = dram("ident_h", [128, 128], "ExternalInput")
    ones1_h = dram("ones1_h", [1, 128], "ExternalInput")
    onesc_h = dram("onesc_h", [128, 1], "ExternalInput")
    onesb_h = dram("onesb_h", [128, 1], "ExternalInput", BF16)
    rotT_h = dram("rotT_h", [128, 128], "ExternalInput")
    masks_h = dram("masks_h", [128, 8 * 4 * 128], "ExternalInput", BF16)
    negb_h = dram("negb_h", [128, 1], "ExternalInput")
    sel_h = dram("sel_h", [128, 2], "ExternalInput")
    xfm_scr = dram("xfm_scr", [E, NT])
    x2_scr = dram("x2_scr", [E, NT])
    kag_in = dram("kag_in", [2 * E, NT], dt=BF16)
    kag_out = dram("kag_out", [8 * E, NT], dt=BF16)
    vag_in = dram("vag_in", [2 * H * NT, 128], dt=BF16)
    vag_out = dram("vag_out", [8 * H * NT, 128], dt=BF16)
    mod_in = dram("mod_in", [128, 192])
    mod_out = dram("mod_out", [128, 192])
    h3_scr = dram("h3_scr", [2 * INTER, NT], dt=BF16)
    out_fm = dram("out_fm", [E, NT], "ExternalOutput")

    plan = Plan()
    P = plan.step
    groups_kv = [[0, 1, 2, 3], [4, 5, 6, 7]]
    group_all = [[0, 1, 2, 3, 4, 5, 6, 7]]

    stack = contextlib.ExitStack()

    def SB(n, sh, dt=F32):
        return stack.enter_context(nc.sbuf_tensor("s_" + n, sh, dt))

    PS = lambda n: stack.enter_context(nc.psum_tensor("p_" + n, [128, 1024], F32))

    ident = SB("ident", [128, 128]); ones1 = SB("ones1", [1, 128])
    onesc = SB("onesc", [128, 1]); rotT = SB("rotT", [128, 128])
    onesb = SB("onesb", [128, 1], BF16)
    cosb = SB("cos", [128, NT]); sinb = SB("sin", [128, NT])
    masks = SB("masks", [128, 4096], BF16)
    negb = SB("negb", [128, 1]); selb = SB("sel", [128, 2])
    ctsl = SB("ctsl", [128, 4]); silu_sl = SB("silu", [128, 4])
    modfm = SB("modfm", [128, 192]); ownmod = SB("ownmod", [128, 96])
    modd = SB("modd", [128, 64]); gwork = SB("gwork", [128, 32])
    hbuf = SB("hbuf", [128, 2 * NCH * NT], BF16)
    qbuf = SB("qbuf", [128, 2 * H * NT], BF16)
    wsl = SB("wsl", [128, 2048])
    wsb = SB("wsb", [128, 4096], BF16)
    kvb = SB("kvb", [128, 2816], BF16)
    esl = SB("esl", [128, 2048])
    eslb = SB("eslb", [128, 3072], BF16)
    h3sb = eslb
    ktb = eslb[:, 0:2048]
    vtb = eslb[:, 2048:3072]
    tmpa = SB("tmpa", [128, 2048])
    scrb = SB("scrb", [128, 2048])
    rows = SB("rows", [1, 2048])
    vt4 = scrb[:, 0:512]
    invb = tmpa[:, 0:1024]
    scol = SB("scol", [128, 8]); sqcol = SB("sqcol", [128, 8])
    modp = scrb[:, 1024:1216]

    psA, psB, psC, psZ = PS("A"), PS("B"), PS("C"), PS("Z")
    PSMAP = [psA, psA, psB, psB, psC, psC, psZ, psZ]

    rbank = [Res() for _ in range(8)]
    r_h = [Res() for _ in range(NCH)]
    r_q = [Res() for _ in range(H)]
    r_wsl = [Res() for _ in range(4)]
    r_wsb = [Res() for _ in range(4)]
    r_k = [Res() for _ in range(4)]; r_v = [Res() for _ in range(7)]
    r_esl = [Res(), Res()]
    r_eb = [Res(), Res()]
    r_tmpa = [Res(), Res()]
    r_scr = Res()
    r_rows = Res(); r_scol = Res(); r_sqcol = Res()
    r_consts = Res(); r_mod = Res(); r_own = Res(); r_modd = Res(); r_gw = Res()
    r_silu = Res()
    d_xfm = [Res(), Res()]; d_x2 = Res()
    d_kin = Res(); d_kout = Res(); d_vin = [Res(), Res()]; d_vout = Res()
    d_modin = Res(); d_modout = Res(); d_h3 = [Res(), Res()]; d_out = Res()

    def bank_ap(k, p0=0, p1=128, c0=0, c1=512):
        return PSMAP[k][p0:p1, (k % 2) * 512 + c0:(k % 2) * 512 + c1]

    def wsb_hi(sl, c0, c1):
        return wsb[:, sl * 1024 + c0:sl * 1024 + c1]

    def wsb_lo(sl, c0, c1):
        return wsb[:, sl * 1024 + 512 + c0:sl * 1024 + 512 + c1]

    def split_w(sl, width):
        P("dve", lambda e, sl=sl, width=width: e.tensor_copy(
            wsb_hi(sl, 0, width), wsl[:, sl * 512:sl * 512 + width]),
          reads=[r_wsl[sl]], writes=[r_wsb[sl]])
        P("dve", lambda e, sl=sl, width=width: e.tensor_sub(
            wsb_lo(sl, 0, width), wsl[:, sl * 512:sl * 512 + width], wsb_hi(sl, 0, width)),
          reads=[r_wsl[sl]], writes=[r_wsb[sl]])

    for dst, src in [(ident[:], ident_h[:]), (ones1[:], ones1_h[:]),
                     (onesc[:], onesc_h[:]), (onesb[:], onesb_h[:]),
                     (rotT[:], rotT_h[:]),
                     (cosb[:], cos_h[:]), (sinb[:], sin_h[:]),
                     (masks[:], masks_h[:]), (negb[:], negb_h[:]),
                     (selb[:], sel_h[:])]:
        P("syncd", lambda e, a=dst, b=src: e.dma_start(out=a, in_=b),
          writes=[r_consts], sem="dconst")
    for kc in range(2):
        P("gpsd", lambda e, kc=kc: e.dma_start(out=ctsl[:, kc * 2:kc * 2 + 2],
                                               in_=ct_sl[kc * 128:(kc + 1) * 128, :]),
          writes=[r_consts], sem="dct")

    P("act", lambda e: e.activation(silu_sl[:], ctsl[:], AF.Silu),
      reads=[r_consts], writes=[r_silu])
    for j in range(96):
        sl = j % 4
        P("syncd", lambda e, j=j, sl=sl: e.dma_start(
            out=wsl[:, sl * 512:sl * 512 + 256],
            in_=wmod_sl[:, j * 128:(j + 1) * 128].rearrange("(kc p) j -> p kc j", p=128)),
          writes=[r_wsl[sl]], sem="dw%d" % sl)

        def modmm(e, j=j, sl=sl):
            pt = psA if j < 48 else psB
            col = (j % 48) * 2
            for kc in range(2):
                ins = e.matmul(pt[:, col:col + 2],
                               wsl[:, sl * 512 + kc * 128:sl * 512 + (kc + 1) * 128],
                               silu_sl[:, kc * 2:kc * 2 + 2],
                               start=(kc == 0), stop=(kc == 1))
            return ins
        P("te", modmm, reads=[r_wsl[sl], r_silu], writes=[rbank[0 if j < 48 else 2]])
    P("dve", lambda e: e.tensor_copy(modp[:, 0:96], psA[:, 0:96]),
      reads=[rbank[0]], writes=[r_scr])
    P("dve", lambda e: e.tensor_copy(modp[:, 96:192], psB[:, 0:96]),
      reads=[rbank[2]], writes=[r_scr])
    P("gpsd", lambda e: e.dma_start(out=mod_in[:], in_=modp[:]),
      reads=[r_scr], writes=[d_modin], sem="dmodin")
    P("gps", lambda e: e.collective_compute(
        "AllReduce", AL.add, replica_groups=group_all,
        ins=[mod_in[:]], outs=[mod_out[:]]),
      reads=[d_modin], writes=[d_modout])
    P("gpsd", lambda e: e.dma_start(out=modfm[:], in_=mod_out[:]),
      reads=[d_modout], writes=[r_mod], sem="dmodld")
    P("dve", lambda e: e.tensor_scalar(scrb[:, 0:96], modfm[:, 0:192:2],
                                       selb[:, 0:1], None, op0=AL.mult),
      reads=[r_mod, r_consts], writes=[r_scr])
    P("dve", lambda e: e.scalar_tensor_tensor(ownmod[:], modfm[:, 1:192:2],
                                              selb[:, 1:2], scrb[:, 0:96],
                                              op0=AL.mult, op1=AL.add),
      reads=[r_mod, r_consts, r_scr], writes=[r_own])
    P("dve", lambda e: e.tensor_scalar_add(modd[:, 0:16], ownmod[:, 16:32], 1.0),
      reads=[r_own], writes=[r_modd])
    P("dve", lambda e: e.tensor_scalar_add(modd[:, 16:32], ownmod[:, 64:80], 1.0),
      reads=[r_own], writes=[r_modd])
    for qq, base in ((2, 32), (5, 48)):
        P("dve", lambda e, qq=qq: e.tensor_single_scalar(gwork[:, 0:16], ownmod[:, qq * 16:(qq + 1) * 16], 0.0, op=AL.is_ge),
          reads=[r_own], writes=[r_gw])
        P("dve", lambda e: e.tensor_scalar(gwork[:, 0:16], gwork[:, 0:16], 2.0, -1.0, op0=AL.mult, op1=AL.add),
          reads=[r_gw], writes=[r_gw])
        P("act", lambda e, qq=qq: e.activation(gwork[:, 16:32], ownmod[:, qq * 16:(qq + 1) * 16], AF.Abs),
          reads=[r_own], writes=[r_gw])
        P("dve", lambda e: e.tensor_scalar_max(gwork[:, 16:32], gwork[:, 16:32], 0.25),
          reads=[r_gw], writes=[r_gw])
        P("dve", lambda e, base=base: e.tensor_mul(modd[:, base:base + 16], gwork[:, 0:16], gwork[:, 16:32]),
          reads=[r_gw], writes=[r_modd])

    sa_ap = lambda ch: ownmod[:, ch:ch + 1]
    sb_ap = lambda ch: ownmod[:, 48 + ch:48 + ch + 1]
    ca1_ap = lambda ch: modd[:, ch:ch + 1]
    cb1_ap = lambda ch: modd[:, 16 + ch:16 + ch + 1]
    ga_ap = lambda ch: modd[:, 32 + ch:32 + ch + 1]
    gb_ap = lambda ch: modd[:, 48 + ch:48 + ch + 1]

    for til in range(8):
        xs = esl[:, 0:2048]
        P("actd", lambda e, til=til, xs=xs: e.dma_start(out=xs, in_=x_own[til * 128:(til + 1) * 128, :]),
          writes=[r_esl[0], r_esl[1]], sem="dx0")
        P("dve", lambda e, til=til, xs=xs: e.reduce_sum(scol[:, til:til + 1], xs, axis=AX),
          reads=[r_esl[0], r_esl[1]], writes=[r_scol])
        P("act", lambda e, til=til, xs=xs: e.activation(scrb[:], xs, AF.Square, accum_out=sqcol[:, til:til + 1]),
          reads=[r_esl[0], r_esl[1]], writes=[r_scr, r_sqcol])
        for bb in range(4):
            def trn(e, til=til, xs=xs, bb=bb):
                for k in range(4):
                    ch = bb * 4 + k
                    ins = e.transpose(bank_ap(bb % 4, c0=k * 128, c1=(k + 1) * 128),
                                      xs[:, ch * 128:(ch + 1) * 128], ident[:])
                return ins
            P("te", trn, reads=[r_esl[0], r_esl[1], r_consts], writes=[rbank[bb % 4]])
            P("dve", lambda e, bb=bb: e.tensor_copy(tmpa[:, (bb % 2) * 1024:(bb % 2) * 1024 + 512],
                                                    bank_ap(bb % 4)),
              reads=[rbank[bb % 4]], writes=[r_tmpa[bb % 2]])
            for k in range(4):
                ch = bb * 4 + k
                P("gpsd", lambda e, til=til, ch=ch, bb=bb, k=k: e.dma_start(
                    out=xfm_scr[ch * 128:(ch + 1) * 128, til * 128:(til + 1) * 128],
                    in_=tmpa[:, (bb % 2) * 1024 + k * 128:(bb % 2) * 1024 + (k + 1) * 128]),
                  reads=[r_tmpa[bb % 2]], writes=[d_xfm[bb % 2]], sem="dxfm%d" % (bb % 2), waw=False)
    for til in range(8):
        P("te", lambda e, til=til: e.transpose(bank_ap(4 + til // 4, p1=1, c0=(til % 4) * 128, c1=(til % 4 + 1) * 128),
                                               scol[:, til:til + 1], ident[:]),
          reads=[r_scol, r_consts], writes=[rbank[4 + til // 4]])
        P("te", lambda e, til=til: e.transpose(bank_ap(6 + til // 4, p1=1, c0=(til % 4) * 128, c1=(til % 4 + 1) * 128),
                                               sqcol[:, til:til + 1], ident[:]),
          reads=[r_sqcol, r_consts], writes=[rbank[6 + til // 4]])

    def stats_rows(sum_banks, sq_banks, sum_src, sq_src):
        P("act", lambda e: e.mul(rows[:, 0:1024], sum_src, 1.0 / 2048.0),
          reads=sum_banks, writes=[r_rows])
        P("act", lambda e: e.mul(rows[:, 1024:2048], sq_src, 1.0 / 2048.0),
          reads=sq_banks, writes=[r_rows])
        P("act", lambda e: e.activation(scrb[0:1, 0:1024], rows[:, 0:1024], AF.Square),
          reads=[r_rows], writes=[r_scr])
        P("dve", lambda e: e.tensor_sub(rows[:, 1024:2048], rows[:, 1024:2048], scrb[0:1, 0:1024]),
          reads=[r_scr], writes=[r_rows])
        P("dve", lambda e: e.tensor_scalar_add(rows[:, 1024:2048], rows[:, 1024:2048], 1e-5),
          reads=[r_rows], writes=[r_rows])
        P("act", lambda e: e.activation(scrb[0:1, 0:1024], rows[:, 1024:2048], AF.Sqrt),
          reads=[r_rows], writes=[r_scr])
        P("dve", lambda e: e.reciprocal(rows[:, 1024:2048], scrb[0:1, 0:1024]),
          reads=[r_scr], writes=[r_rows])

    stats_rows([rbank[4], rbank[5]], [rbank[6], rbank[7]],
               psC[0:1, 0:1024], psZ[0:1, 0:1024])

    def bcast_rows():
        for half in range(2):
            P("te", lambda e, half=half: e.matmul(
                psA[:, half * 512:(half + 1) * 512], ones1[:],
                rows[0:1, half * 512:(half + 1) * 512], start=True, stop=True),
              reads=[r_rows, r_consts], writes=[rbank[half]])
            P("te", lambda e, half=half: e.matmul(
                psB[:, half * 512:(half + 1) * 512], ones1[:],
                rows[0:1, 1024 + half * 512:1024 + (half + 1) * 512], start=True, stop=True),
              reads=[r_rows, r_consts], writes=[rbank[2 + half]])

    def ln_apply(c1ap, shap, dst_res, dst_hi, dst_lo, src_res, src_ap):
        P("dve", lambda e: e.tensor_sub(scrb[:, 0:1024], src_ap, psA[:, 0:1024]),
          reads=[src_res, rbank[0], rbank[1]], writes=[r_scr])
        P("dve", lambda e: e.tensor_mul(scrb[:, 1024:2048], scrb[:, 0:1024], psB[:, 0:1024]),
          reads=[r_scr, rbank[2], rbank[3]], writes=[r_scr])
        P("dve", lambda e: e.tensor_scalar(scrb[:, 0:1024], scrb[:, 1024:2048], c1ap, shap,
                                           op0=AL.mult, op1=AL.add),
          reads=[r_scr, r_modd, r_own], writes=[r_scr])
        P("dve", lambda e: e.tensor_copy(dst_hi, scrb[:, 0:1024]),
          reads=[r_scr], writes=[dst_res])
        P("dve", lambda e: e.tensor_sub(dst_lo, scrb[:, 0:1024], dst_hi),
          reads=[r_scr], writes=[dst_res])

    bcast_rows()
    for ch in range(NCH):
        sl = ch % 2
        xs = esl[:, sl * 1024:(sl + 1) * 1024]
        P("syncd", lambda e, ch=ch, xs=xs: e.dma_start(out=xs, in_=xfm_scr[ch * 128:(ch + 1) * 128, :]),
          reads=d_xfm, writes=[r_esl[sl]], sem="de%ds" % sl)
        ln_apply(ca1_ap(ch), sa_ap(ch), r_h[ch],
                 hbuf[:, ch * 2048:ch * 2048 + 1024],
                 hbuf[:, ch * 2048 + 1024:ch * 2048 + 2048],
                 r_esl[sl], xs)

    def mm3(e, out_ap, whi, wlo, xhi, xlo, first, last):
        e.matmul(out_ap, whi, xhi, start=first, stop=False)
        e.matmul(out_ap, whi, xlo, start=False, stop=False)
        ins = e.matmul(out_ap, wlo, xhi, start=False, stop=last)
        return ins

    def rope_chain(raw_res, raw_ap):
        for half in range(2):
            P("te", lambda e, half=half, raw_ap=raw_ap: e.matmul(
                bank_ap(6 + half), rotT[:], raw_ap[:, half * 512:(half + 1) * 512],
                start=True, stop=True),
              reads=[raw_res, r_consts], writes=[rbank[6 + half]])
        P("dve", lambda e: e.tensor_mul(scrb[:, 0:1024], psZ[:, 0:1024], sinb[:]),
          reads=[rbank[6], rbank[7], r_consts], writes=[r_scr])
        P("dve", lambda e, raw_ap=raw_ap: e.tensor_mul(scrb[:, 1024:2048], raw_ap, cosb[:]),
          reads=[raw_res, r_consts], writes=[r_scr])
        P("dve", lambda e: e.tensor_add(scrb[:, 0:1024], scrb[:, 0:1024], scrb[:, 1024:2048]),
          reads=[r_scr], writes=[r_scr])

    OG_ORDER = [11, 12, 13, 14, 15, 10, 9, 8, 7, 6, 5, 4, 3, 2, 1, 0]
    for og in OG_ORDER:
        ochs = [og * 3 + i for i in range(3)]
        for kch in range(NCH):
            sl = kch % 4
            P("syncd", lambda e, og=og, kch=kch, sl=sl: e.dma_start(
                out=wsl[:, sl * 512:sl * 512 + 384],
                in_=wqkvT[kch * 128:(kch + 1) * 128, og * 384:(og + 1) * 384]),
              writes=[r_wsl[sl]], sem="dw%d" % sl)
            split_w(sl, 384)

            def qkvmm(e, kch=kch, sl=sl):
                for oi in range(3):
                    for half in range(2):
                        ins = mm3(e,
                                  PSMAP[2 * oi][:, half * 512:(half + 1) * 512],
                                  wsb_hi(sl, oi * 128, (oi + 1) * 128),
                                  wsb_lo(sl, oi * 128, (oi + 1) * 128),
                                  hbuf[:, kch * 2048 + half * 512:kch * 2048 + (half + 1) * 512],
                                  hbuf[:, kch * 2048 + 1024 + half * 512:kch * 2048 + 1024 + (half + 1) * 512],
                                  first=(kch == 0), last=(kch == NCH - 1))
                return ins
            P("te", qkvmm, reads=[r_wsb[sl], r_h[kch]],
              writes=[rbank[0], rbank[1], rbank[2], rbank[3], rbank[4], rbank[5]])
        for oi, och in enumerate(ochs):
            asl = oi % 2
            raw = tmpa[:, asl * 1024:(asl + 1) * 1024]
            pX = PSMAP[2 * oi]
            P("dve", lambda e, pX=pX, raw=raw: e.tensor_copy(raw, pX[:, 0:1024]),
              reads=[rbank[2 * oi], rbank[2 * oi + 1]], writes=[r_tmpa[asl]])
            if och < 16:
                hh = och
                rope_chain(r_tmpa[asl], raw)
                P("dve", lambda e, hh=hh: e.tensor_copy(qbuf[:, hh * 2048:hh * 2048 + 1024], scrb[:, 0:1024]),
                  reads=[r_scr], writes=[r_q[hh]])
                P("dve", lambda e, hh=hh: e.tensor_sub(qbuf[:, hh * 2048 + 1024:hh * 2048 + 2048],
                                                       scrb[:, 0:1024], qbuf[:, hh * 2048:hh * 2048 + 1024]),
                  reads=[r_scr], writes=[r_q[hh]])
            elif och < 32:
                hh = och - 16
                rope_chain(r_tmpa[asl], raw)
                P("dve", lambda e: e.tensor_copy(ktb[:, 0:1024], scrb[:, 0:1024]),
                  reads=[r_scr], writes=[r_eb[0]])
                P("dve", lambda e: e.tensor_sub(ktb[:, 1024:2048], scrb[:, 0:1024], ktb[:, 0:1024]),
                  reads=[r_scr], writes=[r_eb[0]])
                P("gpsd", lambda e, hh=hh: e.dma_start(
                    out=kag_in[hh * 256:hh * 256 + 128, :], in_=ktb[:, 0:1024]),
                  reads=[r_eb[0]], writes=[d_kin], sem="dkin", waw=False)
                P("gpsd", lambda e, hh=hh: e.dma_start(
                    out=kag_in[hh * 256 + 128:hh * 256 + 256, :], in_=ktb[:, 1024:2048]),
                  reads=[r_eb[0]], writes=[d_kin], sem="dkin", waw=False)
            else:
                hh = och - 32
                for grp in range(2):
                    def vtr(e, raw=raw, grp=grp):
                        for k in range(4):
                            ins = e.transpose(bank_ap(6 + grp, c0=k * 128, c1=(k + 1) * 128),
                                              raw[:, (grp * 4 + k) * 128:(grp * 4 + k + 1) * 128],
                                              ident[:])
                        return ins
                    P("te", vtr, reads=[r_tmpa[asl], r_consts], writes=[rbank[6 + grp]])
                    P("dve", lambda e, grp=grp: e.tensor_copy(vt4[:], bank_ap(6 + grp)),
                      reads=[rbank[6 + grp]], writes=[r_scr])
                    P("dve", lambda e: e.tensor_copy(vtb[:, 0:512], vt4[:]),
                      reads=[r_scr], writes=[r_eb[1]])
                    P("dve", lambda e: e.tensor_sub(vtb[:, 512:1024], vt4[:], vtb[:, 0:512]),
                      reads=[r_scr], writes=[r_eb[1]])
                    for k in range(4):
                        tid = hh * 8 + grp * 4 + k
                        P("gpsd", lambda e, tid=tid, k=k: e.dma_start(
                            out=vag_in[tid * 256:(tid + 1) * 256, :].rearrange("(two p) c -> p two c", two=2),
                            in_=vtb.rearrange("p (two c) -> p two c", two=2)[:, :, k * 128:(k + 1) * 128]),
                          reads=[r_eb[1]], writes=[d_vin[grp]], sem="dvin%d" % grp, waw=False)
        if og == 10:
            P("gps", lambda e: e.collective_compute(
                "AllGather", AL.bypass, replica_groups=groups_kv,
                ins=[vag_in[:]], outs=[vag_out[:]]),
              reads=d_vin, writes=[d_vout])
        if og == 5:
            P("gps", lambda e: e.collective_compute(
                "AllGather", AL.bypass, replica_groups=groups_kv,
                ins=[kag_in[:]], outs=[kag_out[:]]),
              reads=[d_kin], writes=[d_kout])

    KBLK, VBLK = 2 * E, 2 * H * NT

    def eb_hi(sl3, c0, c1):
        return hbuf[:, sl3 * 2048 + c0:sl3 * 2048 + c1]

    def eb_lo(sl3, c0, c1):
        return hbuf[:, sl3 * 2048 + 1024 + c0:sl3 * 2048 + 1024 + c1]

    for hh in range(H):
        def front(kt, hh=hh):
            w = 128 * (8 - kt // 4)
            par = kt % 2
            sl3 = kt % 5
            ksl, vsl = kt % 4, kt % 7
            rank, col0 = kv_loc(kt)
            P("syncd", lambda e, hh=hh, rank=rank, col0=col0, ksl=ksl: e.dma_start(
                out=kvb[:, ksl * 256:(ksl + 1) * 256],
                in_=kag_out[rank * KBLK + hh * 256:rank * KBLK + (hh + 1) * 256,
                            col0:col0 + 128].rearrange("(two p) c -> p two c", two=2)),
              reads=[d_kout], writes=[r_k[ksl]], sem="dk%d" % ksl)
            P("gpsd", lambda e, hh=hh, rank=rank, col0=col0, vsl=vsl: e.dma_start(
                out=kvb[:, 1024 + vsl * 256:1024 + (vsl + 1) * 256],
                in_=vag_out[rank * VBLK + (hh * 8 + col0 // 128) * 256:
                            rank * VBLK + (hh * 8 + col0 // 128) * 256 + 256, :]
                    .rearrange("(two p) c -> p two c", two=2)),
              reads=[d_vout], writes=[r_v[vsl]], sem="dv%d" % vsl)
            scp = psA if par == 0 else psB
            scb = [rbank[0], rbank[1]] if par == 0 else [rbank[2], rbank[3]]
            nh = 2 if w > 512 else 1

            def score(e, hh=hh, kt=kt, w=w, scp=scp, ksl=ksl):
                for half in range(2 if w > 512 else 1):
                    ww = min(512, w - half * 512)
                    ins = mm3(e, scp[:, half * 512:half * 512 + ww],
                              kvb[:, ksl * 256:ksl * 256 + 128],
                              kvb[:, ksl * 256 + 128:ksl * 256 + 256],
                              qbuf[:, hh * 2048 + half * 512:hh * 2048 + half * 512 + ww],
                              qbuf[:, hh * 2048 + 1024 + half * 512:hh * 2048 + 1024 + half * 512 + ww],
                              first=True, last=True)
                return ins
            P("te", score, reads=[r_k[ksl], r_q[hh]], writes=scb[:nh])
            P("act", lambda e, par=par, w=w, scp=scp: e.activation(
                esl[:, par * 1024:par * 1024 + w], scp[:, 0:w], AF.Exp,
                bias=negb[:, 0:1], scale=SCALE),
              reads=scb[:nh] + [r_consts], writes=[r_esl[par]])
            s_m, j_m = 7 - kt // 4, kt % 4
            P("dve", lambda e, par=par, s_m=s_m, j_m=j_m: e.tensor_mul(
                esl[:, par * 1024 + s_m * 128:par * 1024 + (s_m + 1) * 128],
                esl[:, par * 1024 + s_m * 128:par * 1024 + (s_m + 1) * 128],
                masks[:, (s_m * 4 + j_m) * 128:(s_m * 4 + j_m + 1) * 128]),
              reads=[r_consts], writes=[r_esl[par]])
            P("dve", lambda e, par=par, w=w, sl3=sl3: e.tensor_copy(
                eb_hi(sl3, 0, w), esl[:, par * 1024:par * 1024 + w]),
              reads=[r_esl[par]], writes=[r_h[sl3]])
            P("dve", lambda e, par=par, w=w, sl3=sl3: e.tensor_sub(
                eb_lo(sl3, 0, w), esl[:, par * 1024:par * 1024 + w], eb_hi(sl3, 0, w)),
              reads=[r_esl[par]], writes=[r_h[sl3]])

        def back(kt, hh=hh):
            w = 128 * (8 - kt // 4)
            sl3 = kt % 5
            vsl = kt % 7

            def colsum(e, kt=kt, w=w, sl3=sl3):
                for half in range(2 if w > 512 else 1):
                    ww = min(512, w - half * 512)
                    last = (kt == 31 if half == 0 else kt == 15)
                    e.matmul(psZ[0:1, half * 512:half * 512 + ww], onesb[:],
                             eb_hi(sl3, half * 512, half * 512 + ww),
                             start=(kt == 0), stop=False)
                    ins = e.matmul(psZ[0:1, half * 512:half * 512 + ww], onesb[:],
                                   eb_lo(sl3, half * 512, half * 512 + ww),
                                   start=False, stop=last)
                return ins
            P("te", colsum, reads=[r_h[sl3], r_consts], writes=[rbank[6]] + ([rbank[7]] if w > 512 else []))

            def avmm(e, kt=kt, w=w, sl3=sl3, vsl=vsl):
                for half in range(2 if w > 512 else 1):
                    ww = min(512, w - half * 512)
                    last = (kt == 31 if half == 0 else kt == 15)
                    ins = mm3(e, psC[:, half * 512:half * 512 + ww],
                              kvb[:, 1024 + vsl * 256:1024 + vsl * 256 + 128],
                              kvb[:, 1024 + vsl * 256 + 128:1024 + vsl * 256 + 256],
                              eb_hi(sl3, half * 512, half * 512 + ww),
                              eb_lo(sl3, half * 512, half * 512 + ww),
                              first=(kt == 0), last=last)
                return ins
            P("te", avmm, reads=[r_h[sl3], r_v[vsl]], writes=[rbank[4]] + ([rbank[5]] if w > 512 else []))

        for kt in range(36):
            if kt < 32:
                front(kt)
            if kt >= 4:
                back(kt - 4)
        P("dve", lambda e: e.reciprocal(scrb[0:1, 0:1024], psZ[0:1, 0:1024]),
          reads=[rbank[6], rbank[7]], writes=[r_scr])
        for half in range(2):
            P("te", lambda e, half=half: e.matmul(
                psA[:, half * 512:(half + 1) * 512], ones1[:],
                scrb[0:1, half * 512:(half + 1) * 512], start=True, stop=True),
              reads=[r_scr, r_consts], writes=[rbank[half]])
        P("dve", lambda e: e.tensor_copy(invb[:], psA[:, 0:1024]),
          reads=[rbank[0], rbank[1]], writes=[r_tmpa[0]])
        P("dve", lambda e: e.tensor_mul(tmpa[:, 1024:2048], psC[:, 0:1024], invb[:]),
          reads=[rbank[4], rbank[5], r_tmpa[0]], writes=[r_tmpa[1]])
        P("dve", lambda e, hh=hh: e.tensor_copy(qbuf[:, hh * 2048:hh * 2048 + 1024], tmpa[:, 1024:2048]),
          reads=[r_tmpa[1]], writes=[r_q[hh]])
        P("dve", lambda e, hh=hh: e.tensor_sub(qbuf[:, hh * 2048 + 1024:hh * 2048 + 2048],
                                               tmpa[:, 1024:2048], qbuf[:, hh * 2048:hh * 2048 + 1024]),
          reads=[r_tmpa[1]], writes=[r_q[hh]])

    for og in range(8):
        pbase = 0 if og % 2 == 0 else 4
        for kch in range(NCH):
            sl = kch % 4
            P("syncd", lambda e, og=og, kch=kch, sl=sl: e.dma_start(
                out=wsl[:, sl * 512:sl * 512 + 256],
                in_=woutT[kch * 128:(kch + 1) * 128, og * 256:(og + 1) * 256]),
              writes=[r_wsl[sl]], sem="dw%d" % sl)
            split_w(sl, 256)

            def womm(e, kch=kch, sl=sl, pbase=pbase):
                for oi in range(2):
                    for half in range(2):
                        ins = mm3(e,
                                  PSMAP[pbase + 2 * oi][:, half * 512:(half + 1) * 512],
                                  wsb_hi(sl, oi * 128, (oi + 1) * 128),
                                  wsb_lo(sl, oi * 128, (oi + 1) * 128),
                                  qbuf[:, kch * 2048 + half * 512:kch * 2048 + (half + 1) * 512],
                                  qbuf[:, kch * 2048 + 1024 + half * 512:kch * 2048 + 1024 + (half + 1) * 512],
                                  first=(kch == 0), last=(kch == NCH - 1))
                return ins
            P("te", womm, reads=[r_wsb[sl], r_q[kch]],
              writes=[rbank[pbase + i] for i in range(4)])
        for oi in range(2):
            ech = og * 2 + oi
            sl = ech % 2
            xs = esl[:, sl * 1024:(sl + 1) * 1024]
            P("syncd", lambda e, ech=ech, xs=xs: e.dma_start(out=xs, in_=xfm_scr[ech * 128:(ech + 1) * 128, :]),
              reads=d_xfm, writes=[r_esl[sl]], sem="de%ds" % sl)
            P("dve", lambda e, ech=ech, oi=oi, pbase=pbase, xs=xs: e.scalar_tensor_tensor(
                scrb[:, 0:1024],
                PSMAP[pbase + 2 * oi][:, 0:1024], ga_ap(ech), xs,
                op0=AL.mult, op1=AL.add),
              reads=[rbank[pbase + 2 * oi], rbank[pbase + 2 * oi + 1], r_esl[sl], r_modd],
              writes=[r_scr])
            P("gpsd", lambda e, ech=ech: e.dma_start(
                out=x2_scr[ech * 128:(ech + 1) * 128, :], in_=scrb[:, 0:1024]),
              reads=[r_scr], writes=[d_x2], sem="dx2", waw=False)

    for ch in range(NCH):
        sl = ch % 2
        xs = esl[:, sl * 1024:(sl + 1) * 1024]
        P("syncd", lambda e, ch=ch, xs=xs: e.dma_start(out=xs, in_=x2_scr[ch * 128:(ch + 1) * 128, :]),
          reads=[d_x2], writes=[r_esl[sl]], sem="de%ds" % sl)
        P("te", lambda e, ch=ch, xs=xs: e.matmul(psZ[0:1, 0:512], onesc[:], xs[:, 0:512],
                                                 start=(ch == 0), stop=(ch == NCH - 1)),
          reads=[r_esl[sl], r_consts], writes=[rbank[6]])
        P("te", lambda e, ch=ch, xs=xs: e.matmul(psZ[0:1, 512:1024], onesc[:], xs[:, 512:1024],
                                                 start=(ch == 0), stop=(ch == NCH - 1)),
          reads=[r_esl[sl], r_consts], writes=[rbank[7]])
        P("act", lambda e, ch=ch, xs=xs: e.activation(scrb[:, 0:1024], xs, AF.Square),
          reads=[r_esl[sl]], writes=[r_scr])
        P("te", lambda e, ch=ch: e.matmul(psC[0:1, 0:512], onesc[:], scrb[:, 0:512],
                                          start=(ch == 0), stop=(ch == NCH - 1)),
          reads=[r_scr, r_consts], writes=[rbank[4]])
        P("te", lambda e, ch=ch: e.matmul(psC[0:1, 512:1024], onesc[:], scrb[:, 512:1024],
                                          start=(ch == 0), stop=(ch == NCH - 1)),
          reads=[r_scr, r_consts], writes=[rbank[5]])
    stats_rows([rbank[6], rbank[7]], [rbank[4], rbank[5]],
               psZ[0:1, 0:1024], psC[0:1, 0:1024])
    bcast_rows()
    for ch in range(NCH):
        sl = ch % 2
        xs = esl[:, sl * 1024:(sl + 1) * 1024]
        P("syncd", lambda e, ch=ch, xs=xs: e.dma_start(out=xs, in_=x2_scr[ch * 128:(ch + 1) * 128, :]),
          reads=[d_x2], writes=[r_esl[sl]], sem="de%ds" % sl)
        ln_apply(cb1_ap(ch), sb_ap(ch), r_h[ch],
                 hbuf[:, ch * 2048:ch * 2048 + 1024],
                 hbuf[:, ch * 2048 + 1024:ch * 2048 + 2048],
                 r_esl[sl], xs)

    for i in range(64):
        par = i % 2
        gb_, ub_ = (0, 2) if par == 0 else (4, 6)
        for kch in range(NCH):
            sl = (i * NCH + kch) % 4
            P("syncd", lambda e, i=i, kch=kch, sl=sl: e.dma_start(
                out=wsl[:, sl * 512:sl * 512 + 128],
                in_=wguT[kch * 128:(kch + 1) * 128, i * 128:(i + 1) * 128]),
              writes=[r_wsl[sl]], sem="dw%d" % sl)
            P("syncd", lambda e, i=i, kch=kch, sl=sl: e.dma_start(
                out=wsl[:, sl * 512 + 128:sl * 512 + 256],
                in_=wguT[kch * 128:(kch + 1) * 128, INTER + i * 128:INTER + (i + 1) * 128]),
              writes=[r_wsl[sl]], sem="dw%d" % sl)
            split_w(sl, 256)

            def gumm(e, kch=kch, sl=sl, gb_=gb_, ub_=ub_):
                for half in range(2):
                    xhi = hbuf[:, kch * 2048 + half * 512:kch * 2048 + (half + 1) * 512]
                    xlo = hbuf[:, kch * 2048 + 1024 + half * 512:kch * 2048 + 1024 + (half + 1) * 512]
                    mm3(e, PSMAP[gb_][:, half * 512:(half + 1) * 512],
                        wsb_hi(sl, 0, 128), wsb_lo(sl, 0, 128), xhi, xlo,
                        first=(kch == 0), last=(kch == NCH - 1))
                    ins = mm3(e, PSMAP[ub_][:, half * 512:(half + 1) * 512],
                              wsb_hi(sl, 128, 256), wsb_lo(sl, 128, 256), xhi, xlo,
                              first=(kch == 0), last=(kch == NCH - 1))
                return ins
            P("te", gumm, reads=[r_wsb[sl], r_h[kch]],
              writes=[rbank[gb_], rbank[gb_ + 1], rbank[ub_], rbank[ub_ + 1]])
        sil = tmpa[:, par * 1024:(par + 1) * 1024]
        P("act", lambda e, gb_=gb_, sil=sil: e.activation(sil, PSMAP[gb_][:, 0:1024], AF.Silu),
          reads=[rbank[gb_], rbank[gb_ + 1]], writes=[r_tmpa[par]])
        h3ap = scrb[:, par * 1024:(par + 1) * 1024]
        P("dve", lambda e, ub_=ub_, sil=sil, h3ap=h3ap: e.tensor_mul(h3ap, sil, PSMAP[ub_][:, 0:1024]),
          reads=[r_tmpa[par], rbank[ub_], rbank[ub_ + 1]], writes=[r_scr])
        P("dve", lambda e, par=par, h3ap=h3ap: e.tensor_copy(qbuf[:, par * 2048:par * 2048 + 1024], h3ap),
          reads=[r_scr], writes=[r_q[par]])
        P("dve", lambda e, par=par, h3ap=h3ap: e.tensor_sub(
            qbuf[:, par * 2048 + 1024:par * 2048 + 2048], h3ap, qbuf[:, par * 2048:par * 2048 + 1024]),
          reads=[r_scr], writes=[r_q[par]])
        P("gpsd", lambda e, i=i, par=par: e.dma_start(
            out=h3_scr[i * 128:(i + 1) * 128, :], in_=qbuf[:, par * 2048:par * 2048 + 1024]),
          reads=[r_q[par]], writes=[d_h3[par]], sem="dh3%d" % par, waw=False)
        P("gpsd", lambda e, i=i, par=par: e.dma_start(
            out=h3_scr[INTER + i * 128:INTER + (i + 1) * 128, :],
            in_=qbuf[:, par * 2048 + 1024:par * 2048 + 2048]),
          reads=[r_q[par]], writes=[d_h3[par]], sem="dh3%d" % par, waw=False)
    for eg in range(8):
        pbase = 0 if eg % 2 == 0 else 4
        for ich in range(64):
            wslot = ich % 4
            P("syncd", lambda e, eg=eg, ich=ich, wslot=wslot: e.dma_start(
                out=wsl[:, wslot * 512:wslot * 512 + 256],
                in_=wdT[ich * 128:(ich + 1) * 128, eg * 256:(eg + 1) * 256]),
              writes=[r_wsl[wslot]], sem="dw%d" % wslot)
            split_w(wslot, 256)
            hsl = ich % 8
            P("actd", lambda e, ich=ich, hsl=hsl: e.dma_start(
                out=hbuf[:, hsl * 2048:hsl * 2048 + 1024],
                in_=h3_scr[ich * 128:(ich + 1) * 128, :]),
              reads=d_h3, writes=[r_h[hsl]], sem="de%da" % hsl)
            P("actd", lambda e, ich=ich, hsl=hsl: e.dma_start(
                out=hbuf[:, hsl * 2048 + 1024:hsl * 2048 + 2048],
                in_=h3_scr[INTER + ich * 128:INTER + (ich + 1) * 128, :]),
              reads=d_h3, writes=[r_h[hsl]], sem="de%da" % hsl)

            def dmm(e, ich=ich, wslot=wslot, pbase=pbase, hsl=hsl):
                for oi in range(2):
                    for half in range(2):
                        ins = mm3(e,
                                  PSMAP[pbase + 2 * oi][:, half * 512:(half + 1) * 512],
                                  wsb_hi(wslot, oi * 128, (oi + 1) * 128),
                                  wsb_lo(wslot, oi * 128, (oi + 1) * 128),
                                  hbuf[:, hsl * 2048 + half * 512:hsl * 2048 + (half + 1) * 512],
                                  hbuf[:, hsl * 2048 + 1024 + half * 512:hsl * 2048 + 1024 + (half + 1) * 512],
                                  first=(ich == 0), last=(ich == 63))
                return ins
            P("te", dmm, reads=[r_wsb[wslot], r_h[hsl]],
              writes=[rbank[pbase + i] for i in range(4)])
        for oi in range(2):
            ech = eg * 2 + oi
            sl = ech % 2
            xs = esl[:, sl * 1024:(sl + 1) * 1024]
            P("syncd", lambda e, ech=ech, xs=xs: e.dma_start(out=xs, in_=x2_scr[ech * 128:(ech + 1) * 128, :]),
              reads=[d_x2], writes=[r_esl[sl]], sem="de%ds" % sl)
            P("dve", lambda e, ech=ech, oi=oi, pbase=pbase, xs=xs: e.scalar_tensor_tensor(
                scrb[:, (oi % 2) * 1024:(oi % 2 + 1) * 1024],
                PSMAP[pbase + 2 * oi][:, 0:1024], gb_ap(ech), xs,
                op0=AL.mult, op1=AL.add),
              reads=[rbank[pbase + 2 * oi], rbank[pbase + 2 * oi + 1], r_esl[sl], r_modd],
              writes=[r_scr])
            P("gpsd", lambda e, ech=ech, oi=oi: e.dma_start(
                out=out_fm[ech * 128:(ech + 1) * 128, :],
                in_=scrb[:, (oi % 2) * 1024:(oi % 2 + 1) * 1024]),
              reads=[r_scr], writes=[d_out], sem="dout", waw=False)

    P("gps", lambda e: e.engine_nop(), reads=[d_out])

    sems = {}
    with contextlib.ExitStack() as st2:
        for s in plan.semnames:
            sems[s] = st2.enter_context(nc.semaphore("sem_" + s))
        with nc.Block() as block:
            block.tensor(plan.build_body(sems, "tensor"))
            block.vector(plan.build_body(sems, "vector"))
            block.scalar(plan.build_body(sems, "scalar"))
            block.gpsimd(plan.build_body(sems, "gpsimd"))
            block.sync(plan.build_body(sems, "sync"))
    stack.close()
    return nc


_NC_CACHE = [None]


def _host_tables():
    import jax
    import jax.numpy as jnp
    cpu = jax.devices("cpu")[0]
    with jax.default_device(cpu):
        inv_freq = 1.0 / 10000.0 ** (jnp.arange(0, D, 2, dtype=jnp.float32) / D)
        positions = jnp.arange(S, dtype=jnp.float32)
        freqs = jnp.outer(positions, inv_freq)
        emb = jnp.concatenate([freqs, freqs], axis=-1)
        cos = np.asarray(jnp.cos(emb), dtype=np.float32)
        sin = np.asarray(jnp.sin(emb), dtype=np.float32)
    return cos, sin


def kernel(L0_attn_x, L0_attn_Wqkv, L0_attn_Wout, L0_ffn_w_gate_up,
           L0_ffn_w_down, L0_c, L0_wmod):
    import ml_dtypes
    x = np.ascontiguousarray(np.asarray(L0_attn_x, dtype=np.float32))
    wqkvT = np.ascontiguousarray(np.asarray(L0_attn_Wqkv, np.float32).T)
    woutT = np.ascontiguousarray(np.asarray(L0_attn_Wout, np.float32).T)
    wguT = np.ascontiguousarray(np.asarray(L0_ffn_w_gate_up, np.float32).T)
    wdT = np.ascontiguousarray(np.asarray(L0_ffn_w_down, np.float32).T)
    cT = np.ascontiguousarray(np.asarray(L0_c, np.float32).T)
    wmod = np.ascontiguousarray(np.asarray(L0_wmod, np.float32))
    cos, sin = _host_tables()
    ident = np.eye(128, dtype=np.float32)
    ones1 = np.ones((1, 128), np.float32)
    onesc = np.ones((128, 1), np.float32)
    onesb = np.ones((128, 1), ml_dtypes.bfloat16)
    rotT = np.zeros((128, 128), np.float32)
    for i in range(64):
        rotT[2 * i + 1, 2 * i] = -1.0
        rotT[2 * i, 2 * i + 1] = 1.0
    negb = np.full((128, 1), SHIFT, np.float32)
    tri = (np.arange(128)[:, None] <= np.arange(128)[None, :]).astype(np.float32)

    if _NC_CACHE[0] is None:
        _NC_CACHE[0] = build_nc()
    nc = _NC_CACHE[0]

    in_maps = []
    for c in range(8):
        b, g = c // 4, c % 4
        order = slot_order(g)
        rows_idx = np.concatenate([np.arange(t * 128, (t + 1) * 128) for t in order])
        x_own = np.ascontiguousarray(x[b][rows_idx])
        cos_fm = np.ascontiguousarray(cos[rows_idx].T)
        sin_fm = np.ascontiguousarray(sin[rows_idx].T)
        masks = np.zeros((128, 8 * 4 * 128), ml_dtypes.bfloat16)
        for s in range(8):
            tg = order[s]
            C = 32 - 4 * s
            for j in range(4):
                kt = C - 4 + j
                if kt < tg:
                    m = np.ones((128, 128), ml_dtypes.bfloat16)
                elif kt == tg:
                    m = tri.astype(ml_dtypes.bfloat16)
                else:
                    m = np.zeros((128, 128), ml_dtypes.bfloat16)
                masks[:, (s * 4 + j) * 128:(s * 4 + j + 1) * 128] = m
        sel = np.zeros((128, 2), np.float32)
        sel[:, 0] = 1.0 - b
        sel[:, 1] = float(b)
        ksl = slice(c * 256, (c + 1) * 256)
        in_maps.append({
            "x_own": x_own, "wqkvT": wqkvT, "woutT": woutT, "wguT": wguT,
            "wdT": wdT, "wmod_sl": np.ascontiguousarray(wmod[ksl, :]),
            "ct_sl": np.ascontiguousarray(cT[ksl, :]),
            "cos_h": cos_fm, "sin_h": sin_fm, "ident_h": ident,
            "ones1_h": ones1, "onesc_h": onesc, "onesb_h": onesb,
            "rotT_h": rotT, "masks_h": masks, "negb_h": negb, "sel_h": sel,
        })

    res = run_bass_kernel_spmd(nc, in_maps, core_ids=list(range(8)))
    out = np.empty((2, 4096, 2048), np.float32)
    for c in range(8):
        b, g = c // 4, c % 4
        ofm = np.asarray(res.results[c]["out_fm"])
        otm = ofm.T
        for s, tg in enumerate(slot_order(g)):
            out[b, tg * 128:(tg + 1) * 128, :] = otm[s * 128:(s + 1) * 128, :]
    return out
